# version 13
# baseline (speedup 1.0000x reference)
import numpy as np
from contextlib import ExitStack

import concourse.bass as bass
import concourse.mybir as mybir
from concourse.bass_utils import run_bass_kernel_spmd

F32 = mybir.dt.float32
BF16 = mybir.dt.bfloat16
FP16 = mybir.dt.float16
U32 = mybir.dt.uint32
U16 = mybir.dt.uint16
I32 = mybir.dt.int32
AF = mybir.ActivationFunctionType
ALU = mybir.AluOpType
AX = mybir.AxisListType

NCORES = 8
D = 2048
KC = 16
NH = 16
W = 256
T_OWN = 2048
T_ALL = 2304
TT_OWN = 16
TT_ALL = 18
NCHUNK = 8
I_FFN = 5632
IC = 44
GU = 11264
GOC = 88
HOC = 16

SIGN_MASK = 0x80008000 - (1 << 32)
EEM_MASK = 0x0E000E00


class Sem:
    def __init__(self, st, nc, name):
        self.h = st.enter_context(nc.semaphore(name))
        self.n = 0

    def inc(self, call, k=1):
        call.then_inc(self.h, k)
        self.n += k
        return self.n


def build(phases=9, debug=(), nt=4):
    nc = bass.Bass("TRN2", target_bir_lowering=False)

    x_in = nc.dram_tensor("x", [T_ALL, D], F32, kind="ExternalInput").ap()
    w_att = [nc.dram_tensor(n, [D, D], F32, kind="ExternalInput").ap()
             for n in ("wq", "wk", "wv", "wo")]
    gup = nc.dram_tensor("gup", [GU, 256], I32, kind="ExternalInput").ap()
    gus = nc.dram_tensor("gus", [16, GU], FP16, kind="ExternalInput").ap()
    dnp = nc.dram_tensor("dnp", [D, 704], I32, kind="ExternalInput").ap()
    dns = nc.dram_tensor("dns", [48, D], FP16, kind="ExternalInput").ap()
    cq_in = nc.dram_tensor("cq", [128, TT_OWN, 64], F32, kind="ExternalInput").ap()
    sq_in = nc.dram_tensor("sq", [128, TT_OWN, 64], F32, kind="ExternalInput").ap()
    ck_in = nc.dram_tensor("ck", [128, TT_ALL, 64], F32, kind="ExternalInput").ap()
    sk_in = nc.dram_tensor("sk", [128, TT_ALL, 64], F32, kind="ExternalInput").ap()
    m0_in = nc.dram_tensor("m0", [128, 2, 512], F32, kind="ExternalInput").ap()
    mB_in = nc.dram_tensor("mB", [128, 2, 512], F32, kind="ExternalInput").ap()

    out = nc.dram_tensor("out", [T_OWN, D], F32, kind="ExternalOutput").ap()

    def scratch(name, shape, dt):
        kind = "ExternalOutput" if (name in debug) else "Internal"
        return nc.dram_tensor(name, shape, dt, kind=kind).ap()

    a1f = scratch("a1f", [128, TT_ALL, KC, 128], BF16)
    qtm = scratch("qtm", [T_OWN, D], BF16)
    ktm = scratch("ktm", [T_ALL, D], BF16)
    vtm = scratch("vtm", [T_ALL, D], BF16)
    o_f = scratch("o_f", [128, KC, T_OWN], BF16)
    x2m = scratch("x2m", [T_OWN, D], F32)
    wgu_t = scratch("wgu_t", [GOC, 128, KC, 128], BF16)
    wd_t = scratch("wd_t", [HOC, 128, IC, 128], BF16)

    x_til = x_in.rearrange("(tt p) f -> p tt f", p=128)

    st = ExitStack()
    with st:
        e = st.enter_context
        cst = ExitStack()
        ce = cst.enter_context
        cq = ce(nc.sbuf_tensor("cq_sb", [128, TT_OWN, 64], F32))
        sq = ce(nc.sbuf_tensor("sq_sb", [128, TT_OWN, 64], F32))
        ck = ce(nc.sbuf_tensor("ck_sb", [128, TT_ALL, 64], F32))
        sk = ce(nc.sbuf_tensor("sk_sb", [128, TT_ALL, 64], F32))
        m0 = ce(nc.sbuf_tensor("m0_sb", [128, 2, 512], BF16))
        mB = ce(nc.sbuf_tensor("mB_sb", [128, 2, 512], BF16))
        s_gu = ce(nc.sbuf_tensor("sgu_sb", [128, GOC, 16], FP16))
        s_dn = ce(nc.sbuf_tensor("sdn_sb", [128, HOC, 48], FP16))

        s_fin = Sem(st, nc, "s_fin")

        with ExitStack() as pst, nc.Block() as blk:
            pe_ = pst.enter_context
            m0f = pe_(nc.sbuf_tensor("m0f", [128, 2, 512], F32))
            mBf = pe_(nc.sbuf_tensor("mBf", [128, 2, 512], F32))
            gus_sb = pe_(nc.sbuf_tensor("gus_sb", [16, GU], FP16))
            dns_sb = pe_(nc.sbuf_tensor("dns_sb", [48, D], FP16))
            sc_in = Sem(pst, nc, "c_in")
            sc_xb = Sem(pst, nc, "c_xb")

            @blk.sync
            def _(sync):
                for dst, src in ((cq, cq_in), (sq, sq_in), (ck, ck_in), (sk, sk_in),
                                 (m0f, m0_in), (mBf, mB_in), (gus_sb, gus), (dns_sb, dns)):
                    sc_in.inc(sync.dma_start(out=dst[:], in_=src), 16)
                sync.wait_ge(sc_in.h, sc_in.n)
                sc_xb.inc(sync.dma_start_transpose(s_gu[:], gus_sb[:]), 16)
                sc_xb.inc(sync.dma_start_transpose(s_dn[:], dns_sb[:]), 16)
                sync.wait_ge(sc_xb.h, sc_xb.n)

            @blk.vector
            def _(v):
                v.wait_ge(sc_in.h, 128)
                v.tensor_copy(m0[:], m0f[:])
                v.tensor_copy(mB[:], mBf[:])

        mid = ExitStack()
        with mid:
            me = mid.enter_context
            wbuf = [me(nc.sbuf_tensor("wbuf0", [128, KC, D], BF16))]
            wstg = [me(nc.sbuf_tensor(f"wstg{i}", [128, D], BF16))
                    for i in range(2)]
            s_wst = Sem(st, nc, "s_wst")
            s_wxb = Sem(st, nc, "s_wxb")

            def emit_wprep_g(g, widx, j):
                u = widx * KC + j
                if u >= 2:
                    g.wait_ge(s_wxb.h, 16 * (u - 1))
                s_wst.inc(g.dma_start(out=wstg[j % 2][:],
                                      in_=w_att[widx][128 * j:128 * (j + 1), :]), 16)

            def emit_wprep_x(sync, widx, buf, j):
                u = widx * KC + j
                sync.wait_ge(s_wst.h, 16 * (u + 1))
                s_wxb.inc(sync.dma_start_transpose(
                    wbuf[buf][:, :, 128 * j:128 * (j + 1)], wstg[j % 2][:]), 16)

            if phases >= 1:
                with ExitStack() as pst, nc.Block() as blk:
                    pe_ = pst.enter_context
                    xin = [pe_(nc.sbuf_tensor(f"p0x{i}", [128, D], F32)) for i in range(2)]
                    a1t = [pe_(nc.sbuf_tensor(f"p0a{i}", [128, D], BF16)) for i in range(2)]
                    xbo = [pe_(nc.sbuf_tensor(f"p0b{i}", [128, KC, 128], BF16)) for i in range(2)]
                    bn6 = [pe_(nc.sbuf_tensor(f"p0n{i}", [128, 24], F32)) for i in range(2)]
                    mv = [pe_(nc.sbuf_tensor(f"p0m{i}", [128, 2], F32)) for i in range(2)]
                    vep = [pe_(nc.sbuf_tensor(f"p0v{i}", [128, 1], F32)) for i in range(2)]
                    ngm = [pe_(nc.sbuf_tensor(f"p0g{i}", [128, 1], F32)) for i in range(2)]
                    alp = [pe_(nc.sbuf_tensor(f"p0A{i}", [128, 1], F32)) for i in range(2)]
                    bet = [pe_(nc.sbuf_tensor(f"p0B{i}", [128, 1], F32)) for i in range(2)]
                    lnt = [pe_(nc.sbuf_tensor(f"p0L{i}", [128, 1], F32)) for i in range(2)]
                    p_x = Sem(pst, nc, "p0_x")
                    p_fv = Sem(pst, nc, "p0_fv")
                    p_fs = Sem(pst, nc, "p0_fs")
                    p_st = Sem(pst, nc, "p0_st")
                    p_nm = Sem(pst, nc, "p0_nm")
                    p_xb = Sem(pst, nc, "p0_xb")
                    p_so = Sem(pst, nc, "p0_so")

                    @blk.gpsimd
                    def _(g):
                        for tt in range(TT_ALL):
                            b = tt % 2
                            if tt >= 2:
                                g.wait_ge(p_nm.h, tt - 1)
                            p_x.inc(g.dma_start(out=xin[b][:], in_=x_til[:, tt, :]), 16)
                            if tt < KC:
                                emit_wprep_g(g, 0, tt)

                    @blk.vector
                    def _(v):
                        for tt in range(TT_ALL):
                            b = tt % 2
                            v.wait_ge(p_x.h, 16 * (tt + 1))
                            last = None
                            for q4 in range(4):
                                last = v.bn_stats(bn6[b][:, 6 * q4:6 * (q4 + 1)],
                                                  xin[b][:, 512 * q4:512 * (q4 + 1)])
                            p_fv.inc(last, 1); v.wait_ge(p_fv.h, p_fv.n)
                            p_fv.inc(v.bn_aggr(mv[b][:], bn6[b][:]), 1); v.wait_ge(p_fv.h, p_fv.n)
                            v.tensor_scalar(ngm[b][:], mv[b][:, 0:1], -1.0, None, ALU.mult)
                            p_st.inc(v.tensor_scalar(vep[b][:], mv[b][:, 1:2], 1e-5, None, ALU.add), 1)

                    @blk.scalar
                    def _(sc):
                        for tt in range(TT_ALL):
                            b = tt % 2
                            sc.wait_ge(p_st.h, tt + 1)
                            p_fs.inc(sc.activation(lnt[b][:], vep[b][:], AF.Ln, bias=0.0, scale=1.0), 1)
                            sc.wait_ge(p_fs.h, p_fs.n)
                            p_fs.inc(sc.activation(alp[b][:], lnt[b][:], AF.Exp, bias=0.0, scale=-0.5), 1)
                            sc.wait_ge(p_fs.h, p_fs.n)
                            p_fs.inc(sc.activation(bet[b][:], ngm[b][:], AF.Identity, bias=0.0, scale=alp[b][:]), 1)
                            sc.wait_ge(p_fs.h, p_fs.n)
                            if tt >= 2:
                                sc.wait_ge(p_xb.h, 16 * (tt - 1))
                            p_nm.inc(sc.activation(a1t[b][:], xin[b][:], AF.Identity,
                                                   bias=bet[b][:], scale=alp[b][:]), 1)

                    @blk.sync
                    def _(sync):
                        for tt in range(TT_ALL):
                            b = tt % 2
                            sync.wait_ge(p_nm.h, tt + 1)
                            if tt >= 2:
                                sync.wait_ge(p_so.h, 16 * (tt - 1))
                            p_xb.inc(sync.dma_start_transpose(xbo[b][:], a1t[b][:]), 16)
                            sync.wait_ge(p_xb.h, 16 * (tt + 1))
                            p_so.inc(sync.dma_start(out=a1f[:, tt], in_=xbo[b][:]), 16)
                            if tt < KC:
                                emit_wprep_x(sync, 0, 0, tt)
                        sync.wait_ge(p_so.h, 16 * TT_ALL)

            wbuf.append(me(nc.sbuf_tensor("wbuf1", [128, KC, D], BF16)))

            dq_w = [me(nc.sbuf_tensor(f"dqw{i}", [128, 256], U32)) for i in range(2)]
            dq_A = me(nc.sbuf_tensor("dqA", [128, 1408], U16))
            dq_B = me(nc.sbuf_tensor("dqB", [128, 1408], U16))
            s_ex = me(nc.sbuf_tensor("dqs", [128, 1408], FP16))
            dqq = [me(nc.sbuf_tensor(f"dqq{i}", [128, 2048], BF16)) for i in range(2)]
            dq_o = [me(nc.sbuf_tensor(f"dqo{i}", [128, KC, 128], BF16)) for i in range(2)]
            dn_buf = {}
            d_in = Sem(st, nc, "d_in")
            d_v = Sem(st, nc, "d_v")
            d_xb = Sem(st, nc, "d_xb")
            d_so = Sem(st, nc, "d_so")
            DQTOT = GOC + HOC

            def dq_spec(i):
                return ("gu", i) if i < GOC else ("dn", i - GOC)

            def pd_g(g, i):
                if i >= DQTOT:
                    return
                kind, idx = dq_spec(i)
                if kind == "gu":
                    if i >= 2:
                        g.wait_ge(d_v.h, i - 1)
                    d_in.inc(g.dma_start(
                        out=dq_w[i % 2][:, 0:256],
                        in_=gup[128 * idx:128 * (idx + 1), :].bitcast(U32)), 16)
                else:
                    if i >= 1:
                        g.wait_ge(d_v.h, i)
                    d_in.inc(g.dma_start(
                        out=dn_buf["w"][:],
                        in_=dnp[128 * idx:128 * (idx + 1), :].bitcast(U32)), 16)

            def pd_gs(g, i):
                if i >= DQTOT:
                    return
                kind, idx = dq_spec(i)
                g.wait_ge(d_xb.h, 16 * (i + 1))
                if kind == "gu":
                    d_so.inc(g.dma_start(out=wgu_t[idx], in_=dq_o[i % 2][:]), 16)
                else:
                    d_so.inc(g.dma_start(out=wd_t[idx], in_=dn_buf["o"][i % 2][:]), 16)

            def pd_v(v, i):
                if i >= DQTOT:
                    return
                kind, idx = dq_spec(i)
                nw = 256 if kind == "gu" else 704
                nq = 2 * nw
                ng = nq // 32
                v.wait_ge(d_in.h, 16 * (i + 1))
                need = i if kind == "dn" else i - 1
                if need >= 1:
                    v.wait_ge(d_xb.h, 16 * need)
                if kind == "gu":
                    s_src = s_gu[:, idx, :].unsqueeze(2).broadcast_to([128, ng, 32])
                    wv = dq_w[i % 2][:].bitcast(U16)
                    qq = dqq[i % 2][:]
                    A, B, sx = dq_A, dq_B, s_ex
                else:
                    s_src = s_dn[:, idx, 0:44].unsqueeze(2).broadcast_to([128, ng, 32])
                    wv = dn_buf["w"][:].bitcast(U16)
                    qq = dn_buf["q"][:]
                    A, B, sx = dn_buf["A"], dn_buf["B"], dn_buf["sx"]
                v.tensor_scalar(sx[:, 0:nq].rearrange("p (g x) -> p g x", g=ng),
                                s_src, 16384.0, None, ALU.mult)
                for j in range(4):
                    if j < 3:
                        v.tensor_scalar(A[:, 0:nq], wv, 9 - 4 * j, 0x0E00,
                                        ALU.logical_shift_left, ALU.bitwise_and)
                        v.tensor_scalar(B[:, 0:nq], wv, 12 - 4 * j, 0x8000,
                                        ALU.logical_shift_left, ALU.bitwise_and)
                    else:
                        v.tensor_scalar(A[:, 0:nq], wv, 3, 0x0E00,
                                        ALU.logical_shift_right, ALU.bitwise_and)
                        v.tensor_scalar(B[:, 0:nq], wv, 0x8000, None,
                                        ALU.bitwise_and)
                    v.tensor_tensor(A[:, 0:nq], A[:, 0:nq], B[:, 0:nq],
                                    ALU.bitwise_or)
                    last = v.tensor_tensor(qq[:, j * nq:(j + 1) * nq],
                                           A[:, 0:nq].bitcast(FP16),
                                           sx[:, 0:nq], ALU.mult)
                d_v.inc(last, 1)

            def pd_s(sync, i):
                if i >= DQTOT:
                    return
                kind, idx = dq_spec(i)
                sync.wait_ge(d_v.h, i + 1)
                if i >= 2:
                    sync.wait_ge(d_so.h, 16 * (i - 1))
                if kind == "gu":
                    d_xb.inc(sync.dma_start_transpose(dq_o[i % 2][:], dqq[i % 2][:]), 16)
                else:
                    d_xb.inc(sync.dma_start_transpose(dn_buf["o"][i % 2][:],
                                                      dn_buf["q"][:]), 16)

            PD_P1 = lambda ti: [2 * ti, 2 * ti + 1]
            PD_P2 = lambda ti: [32 + 2 * ti, 32 + 2 * ti + 1]
            PD_P3 = lambda ti: ([68 + 2 * ti, 69 + 2 * ti] if ti < 2
                                else [70 + ti] if ti < 18 else [])

            def proj_phase(pid, wb, prep_widx, prep_buf, tts, rope, out_ap, tt0,
                           wxb_need=0, pd_iters=lambda ti: []):
                with ExitStack() as pst, nc.Block() as blk:
                    pe_ = pst.enter_context
                    a1s = [pe_(nc.sbuf_tensor(f"q{pid}a{i}", [128, KC, 128], BF16)) for i in range(2)]
                    osb = [pe_(nc.sbuf_tensor(f"q{pid}o{i}", [128, D], BF16)) for i in range(2)]
                    tmp = [pe_(nc.sbuf_tensor(f"q{pid}t{i}", [128, 4, 64], F32)) for i in range(2)]
                    ps = [pe_(nc.psum_tensor(f"q{pid}p{i}", [128, D], F32)) for i in range(2)]
                    p_x = Sem(pst, nc, f"q{pid}_x")
                    p_pe = Sem(pst, nc, f"q{pid}_pe")
                    p_ev = Sem(pst, nc, f"q{pid}_ev")
                    p_so = Sem(pst, nc, f"q{pid}_so")
                    ot = out_ap.rearrange("(tt p) f -> p tt f", p=128)
                    n = len(tts)

                    @blk.gpsimd
                    def _(g):
                        for ti in range(n):
                            b = ti % 2
                            if ti >= 2:
                                g.wait_ge(p_pe.h, 4 * (ti - 1))
                            p_x.inc(g.dma_start(out=a1s[b][:], in_=a1f[:, tts[ti]]), 16)
                            if prep_widx is not None and ti < KC:
                                emit_wprep_g(g, prep_widx, ti)
                            for i in pd_iters(ti):
                                pd_g(g, i)
                            if ti >= 1:
                                for i in pd_iters(ti - 1):
                                    pd_gs(g, i)
                        for i in pd_iters(n - 1):
                            pd_gs(g, i)
                        if pid == 2:
                            g.wait_ge(d_so.h, 16 * GOC)

                    @blk.tensor
                    def _(t):
                        t.wait_ge(s_wxb.h, wxb_need)
                        for ti in range(n):
                            b = ti % 2
                            t.wait_ge(p_x.h, 16 * (ti + 1))
                            if ti >= 2:
                                t.wait_ge(p_ev.h, 4 * (ti - 1))
                            for ow in range(4):
                                for kc in range(KC):
                                    mm = t.matmul(ps[b][:, 512 * ow:512 * (ow + 1)],
                                                  a1s[b][:, kc, :],
                                                  wb[:, kc, 512 * ow:512 * (ow + 1)],
                                                  start=(kc == 0), stop=(kc == KC - 1))
                                p_pe.inc(mm, 1)

                    @blk.vector
                    def _(v):
                        for ti in range(n):
                            b = ti % 2
                            for ow in range(4):
                                v.wait_ge(p_pe.h, 4 * ti + ow + 1)
                                if ti >= 2:
                                    v.wait_ge(p_so.h, 16 * (ti - 1))
                                win = ps[b][:, 512 * ow:512 * (ow + 1)].rearrange(
                                    "p (h x) -> p h x", h=4)
                                owin = osb[b][:, 512 * ow:512 * (ow + 1)].rearrange(
                                    "p (h x) -> p h x", h=4)
                                if rope is not None:
                                    c_sb, s_sb, toff = rope
                                    tti = tts[ti] - toff
                                    c_b = c_sb[:, tti:tti + 1, :].broadcast_to([128, 4, 64])
                                    s_b = s_sb[:, tti:tti + 1, :].broadcast_to([128, 4, 64])
                                    ev, od = win[:, :, 0::2], win[:, :, 1::2]
                                    v.tensor_tensor(tmp[0][:], ev, c_b, ALU.mult)
                                    v.tensor_tensor(tmp[1][:], od, s_b, ALU.mult)
                                    v.tensor_tensor(owin[:, :, 0::2], tmp[0][:], tmp[1][:], ALU.subtract)
                                    v.tensor_tensor(tmp[0][:], ev, s_b, ALU.mult)
                                    v.tensor_tensor(tmp[1][:], od, c_b, ALU.mult)
                                    last = v.tensor_tensor(owin[:, :, 1::2], tmp[0][:], tmp[1][:], ALU.add)
                                else:
                                    last = v.tensor_copy(osb[b][:, 512 * ow:512 * (ow + 1)],
                                                         ps[b][:, 512 * ow:512 * (ow + 1)])
                                p_ev.inc(last, 1)
                            for i in pd_iters(ti):
                                pd_v(v, i)

                    @blk.sync
                    def _(sync):
                        for ti in range(n):
                            b = ti % 2
                            if ti >= 1:
                                for i in pd_iters(ti - 1):
                                    pd_s(sync, i)
                            sync.wait_ge(p_ev.h, 4 * (ti + 1))
                            p_so.inc(sync.dma_start(out=ot[:, tts[ti] - tt0, :], in_=osb[b][:]), 16)
                            if prep_widx is not None and ti < KC:
                                emit_wprep_x(sync, prep_widx, prep_buf, ti)
                        for i in pd_iters(n - 1):
                            pd_s(sync, i)
                        sync.wait_ge(p_so.h, 16 * n)

            if phases >= 2:
                proj_phase(0, wbuf[0], 1, 1, list(range(2, TT_ALL)), (cq, sq, 2), qtm, 2,
                           wxb_need=16 * KC, pd_iters=PD_P1)
            if phases >= 3:
                proj_phase(1, wbuf[1], 2, 0, list(range(TT_ALL)), (ck, sk, 0), ktm, 0,
                           wxb_need=32 * KC, pd_iters=PD_P2)
            if phases >= 4:
                proj_phase(2, wbuf[0], None, None, list(range(TT_ALL)), None, vtm, 0,
                           wxb_need=48 * KC, pd_iters=PD_P3)


        if phases >= 6:
            dbg4 = "dbg4" in debug
            if dbg4:
                pr_d = nc.dram_tensor("pr_d", [128, NH, 512], BF16, kind="ExternalOutput").ap()
                pT_d = nc.dram_tensor("pT_d", [128, 64, 128], BF16, kind="ExternalOutput").ap()
                nmx_d = nc.dram_tensor("nmx_d", [128, NH], F32, kind="ExternalOutput").ap()
                sm_d = nc.dram_tensor("sm_d", [128, NH], F32, kind="ExternalOutput").ap()
                qf_d = nc.dram_tensor("qf_d", [128, 2, KC, 128], BF16, kind="ExternalOutput").ap()
                osb_d = nc.dram_tensor("osb_d", [128, NH, 256], BF16, kind="ExternalOutput").ap()
                kf_d = nc.dram_tensor("kf_d", [128, 4, KC, 128], BF16, kind="ExternalOutput").ap()
                vf_d = nc.dram_tensor("vf_d", [128, 4, D], BF16, kind="ExternalOutput").ap()
            with ExitStack() as pst, nc.Block() as blk:
                pe_ = pst.enter_context
                qf = pe_(nc.sbuf_tensor("a_q", [128, 2, KC, 128], BF16))
                kf = pe_(nc.sbuf_tensor("a_k", [128, 4, KC, 128], BF16))
                vf = pe_(nc.sbuf_tensor("a_v", [128, 4, D], BF16))
                pr = [pe_(nc.sbuf_tensor(f"a_p{i}", [128, NH, 512], BF16)) for i in range(2)]
                pr2 = pe_(nc.sbuf_tensor("a_p2", [128, NH, 512], BF16))
                pT = pe_(nc.sbuf_tensor("a_T", [128, 64, 128], BF16))
                osb = [pe_(nc.sbuf_tensor(f"a_o{i}", [128, NH, 256], BF16)) for i in range(2)]
                nmx = pe_(nc.sbuf_tensor("a_nm", [128, NH], F32))
                sm = pe_(nc.sbuf_tensor("a_sm", [128, NH], F32))
                rs = pe_(nc.sbuf_tensor("a_rs", [128, NH], F32))
                dmy = pe_(nc.sbuf_tensor("a_dmy", [128, 1], F32))
                smsnap = pe_(nc.sbuf_tensor("a_smsnap", [128, NH], F32))
                ps_s = pe_(nc.psum_tensor("a_ps", [128, 2048], F32))
                ps_a = [pe_(nc.psum_tensor(f"a_pa{i}", [128, 512], F32)) for i in range(2)]
                a_ld = Sem(pst, nc, "a_ld")
                a_vl = Sem(pst, nc, "a_vl")
                a_sc = Sem(pst, nc, "a_sc")
                a_nm = Sem(pst, nc, "a_nm")
                a_ex = Sem(pst, nc, "a_ex")
                a_st = Sem(pst, nc, "a_st")
                a_nr = Sem(pst, nc, "a_nr")
                a_px = Sem(pst, nc, "a_px")
                a_av = Sem(pst, nc, "a_av")
                a_ev = Sem(pst, nc, "a_ev")
                a_os = Sem(pst, nc, "a_os")
                a_fv = Sem(pst, nc, "a_fv")
                a_dbg = Sem(pst, nc, "a_dbg")
                GH = 8
                dn_buf["w"] = pe_(nc.sbuf_tensor("dnw", [128, 704], U32))
                dn_buf["q"] = pe_(nc.sbuf_tensor("dnq", [128, I_FFN], BF16))
                dn_buf["o"] = [pe_(nc.sbuf_tensor(f"dno{i}", [128, IC, 128], BF16))
                               for i in range(2)]
                dn_buf["A"] = pe_(nc.sbuf_tensor("dnA", [128, 1408], U16))
                dn_buf["B"] = pe_(nc.sbuf_tensor("dnB", [128, 1408], U16))
                dn_buf["sx"] = pe_(nc.sbuf_tensor("dnsx", [128, 1408], FP16))

                @blk.sync
                def _(sync):
                    def pt_xbar(hp):
                        sync.wait_ge(a_nr.h, hp + 1)
                        if hp >= 1:
                            sync.wait_ge(a_av.h, 4 * hp)
                        a_px.inc(sync.dma_start_transpose(
                            pT[:], pr[hp % 2][:].rearrange("p h x -> p (h x)")), 16)

                    for c in range(NCHUNK):
                        for hp in (2 * c - 2, 2 * c - 1):
                            if hp >= 0:
                                pt_xbar(hp)
                        if c >= 1:
                            pd_s(sync, 86 + 2 * c)
                            pd_s(sync, 87 + 2 * c)
                            sync.wait_ge(a_sc.h, 2 * GH * c)
                        for hf in range(2):
                            a_ld.inc(sync.dma_start_transpose(
                                qf[:, hf], qtm[256 * c + 128 * hf:256 * c + 128 * (hf + 1), :]), 16)
                        for hh in range(4):
                            a_ld.inc(sync.dma_start_transpose(
                                kf[:, hh], ktm[256 * c + 128 * hh:256 * c + 128 * (hh + 1), :]), 16)
                        for half in range(2):
                            hp = 2 * c + half
                            if dbg4 and False and c == 2 and half == 0:
                                sync.wait_ge(a_px.h, 16)
                                a_dbg.inc(sync.dma_start(out=pr_d.rearrange("p a b -> p (a b)"),
                                               in_=pr[0][:].rearrange("p a b -> p (a b)")), 16)
                                a_dbg.inc(sync.dma_start(out=pT_d.rearrange("p a b -> p (a b)"),
                                               in_=pT[:].rearrange("p a b -> p (a b)")), 16)
                                a_dbg.inc(sync.dma_start(out=nmx_d, in_=nmx[:]), 16)
                                a_dbg.inc(sync.dma_start(out=sm_d, in_=smsnap[:]), 16)
                                a_dbg.inc(sync.dma_start(out=qf_d.rearrange("p a b c -> p (a b c)"),
                                               in_=qf[:].rearrange("p a b c -> p (a b c)")), 16)
                                a_dbg.inc(sync.dma_start(out=kf_d.rearrange("p a b c -> p (a b c)"),
                                               in_=kf[:].rearrange("p a b c -> p (a b c)")), 16)
                                a_dbg.inc(sync.dma_start(out=vf_d.rearrange("p a b -> p (a b)"),
                                               in_=vf[:].rearrange("p a b -> p (a b)")), 16)
                                sync.wait_ge(a_dbg.h, a_dbg.n)
                    pt_xbar(2 * NCHUNK - 2)
                    pt_xbar(2 * NCHUNK - 1)
                    pd_s(sync, DQTOT - 2)
                    pd_s(sync, DQTOT - 1)

                @blk.gpsimd
                def _(g):
                    for c in range(NCHUNK):
                        if c >= 1:
                            g.wait_ge(a_av.h, 8 * c)
                        a_vl.inc(g.dma_start(
                            out=vf[:],
                            in_=vtm[256 * c:256 * c + 512, :].rearrange("(kvc p) f -> p kvc f", p=128)), 16)
                        pd_g(g, 88 + 2 * c)
                        pd_g(g, 89 + 2 * c)
                        if c >= 1:
                            pd_gs(g, 86 + 2 * c)
                            pd_gs(g, 87 + 2 * c)
                        g.wait_ge(a_ev.h, 8 * (c + 1))
                        if c >= 2:
                            g.wait_ge(a_os.h, 16 * (c - 1))
                        if dbg4 and c == 2:
                            a_dbg.inc(g.dma_start(out=osb_d.rearrange("p a b -> p (a b)"),
                                                  in_=osb[c % 2][:].rearrange("p a b -> p (a b)")), 16)
                        a_os.inc(g.dma_start(out=o_f[:, :, 256 * c:256 * (c + 1)],
                                             in_=osb[c % 2][:]), 16)
                    pd_gs(g, DQTOT - 2)
                    pd_gs(g, DQTOT - 1)
                    g.wait_ge(d_so.h, 16 * DQTOT)
                    g.wait_ge(a_os.h, 16 * NCHUNK)

                @blk.tensor
                def _(t):
                    def emit_av(hp):
                        c = hp // 2
                        t.wait_ge(a_vl.h, 16 * (c + 1))
                        t.wait_ge(a_px.h, 16 * (hp + 1))
                        for g4 in range(4):
                            ga = 4 * hp + g4
                            if ga >= 2:
                                t.wait_ge(a_ev.h, ga - 1)
                            pa = ps_a[g4 % 2]
                            for hh in range(4):
                                h = 4 * g4 + hh
                                for kvc in range(4):
                                    mm = t.matmul(pa[:, 128 * hh:128 * (hh + 1)],
                                                  vf[:, kvc, 128 * h:128 * (h + 1)],
                                                  pT[:, 4 * h + kvc, :],
                                                  start=(kvc == 0), stop=(kvc == 3))
                            a_av.inc(mm, 1)

                    for hp in range(2 * NCHUNK):
                        c, half = divmod(hp, 2)
                        if half == 0:
                            t.wait_ge(a_ld.h, 16 * 6 * (c + 1))
                        for gi in range(GH):
                            gg = GH * hp + gi
                            if gg >= 2:
                                t.wait_ge(a_ex.h, gg - 1)
                            for hh in range(2):
                                h = 2 * gi + hh
                                sl = ps_s[:, 1024 * (gi % 2) + 512 * hh:
                                          1024 * (gi % 2) + 512 * (hh + 1)]
                                t.matmul(sl[:, 0:256], qf[:, half, h, :], kf[:, 0:2, h, :],
                                         start=True, stop=False)
                                mm = t.matmul(sl[:, 256:512], qf[:, half, h, :], kf[:, 2:4, h, :],
                                              start=False, stop=True)
                            a_sc.inc(mm, 1)
                        if hp >= 1:
                            emit_av(hp - 1)
                    emit_av(2 * NCHUNK - 1)

                @blk.vector
                def _(v):
                    def evacs(hp):
                        c, half = divmod(hp, 2)
                        for g4 in range(4):
                            ga = 4 * hp + g4
                            v.wait_ge(a_av.h, ga + 1)
                            if c >= 2 and half == 0 and g4 == 0:
                                v.wait_ge(a_os.h, 16 * (c - 1))
                            a_ev.inc(v.tensor_copy(
                                osb[c % 2][:, 4 * g4:4 * (g4 + 1), 128 * half:128 * (half + 1)],
                                ps_a[g4 % 2][:].rearrange("p (h x) -> p h x", h=4)), 1)

                    for hp in range(2 * NCHUNK):
                        c, half = divmod(hp, 2)
                        msk = m0 if c == 0 else mB
                        for gi in range(GH):
                            gg = GH * hp + gi
                            v.wait_ge(a_sc.h, gg + 1)
                            sl = ps_s[:, 1024 * (gi % 2):1024 * (gi % 2) + 1024]
                            a_nm.inc(v.tensor_reduce(
                                nmx[:, 2 * gi:2 * gi + 2],
                                sl.rearrange("p (h x) -> p h x", h=2),
                                AX.X, ALU.max, negate=True), 1)
                            v.wait_ge(a_ex.h, gg + 1)
                            for hh in range(2):
                                h = 2 * gi + hh
                                last = v.tensor_tensor(
                                    pr2[:, h, :], pr[hp % 2][:, h, :],
                                    msk[:, half, :], ALU.mult)
                        a_fv.inc(last, 1)
                        v.wait_ge(a_fv.h, a_fv.n)
                        a_fv.inc(v.tensor_reduce(sm[:], pr2[:], AX.X, ALU.add), 1)
                        v.wait_ge(a_fv.h, a_fv.n)
                        a_fv.inc(v.reciprocal(rs[:], sm[:]), 1)
                        v.wait_ge(a_fv.h, a_fv.n)
                        for h in range(NH):
                            last = v.tensor_scalar(pr[hp % 2][:, h, :], pr2[:, h, :],
                                                   rs[:, h:h + 1], None, ALU.mult)
                        a_fv.inc(last, 1)
                        v.wait_ge(a_fv.h, a_fv.n)
                        a_nr.inc(v.tensor_copy(dmy[:], rs[:, 0:1]), 1)
                        if hp >= 1:
                            evacs(hp - 1)
                        pd_v(v, 88 + hp)
                    evacs(2 * NCHUNK - 1)

                @blk.scalar
                def _(sc):
                    for hp in range(2 * NCHUNK):
                        if hp >= 2:
                            sc.wait_ge(a_px.h, 16 * (hp - 1))
                        for gi in range(GH):
                            gg = GH * hp + gi
                            sc.wait_ge(a_nm.h, gg + 1)
                            for hh in range(2):
                                h = 2 * gi + hh
                                sl = ps_s[:, 1024 * (gi % 2) + 512 * hh:
                                          1024 * (gi % 2) + 512 * (hh + 1)]
                                last = sc.activation(pr[hp % 2][:, h, :], sl, AF.Exp,
                                                     bias=nmx[:, h:h + 1], scale=1.0)
                            a_ex.inc(last, 1)

        cst.close()

        if phases >= 7:
            with ExitStack() as pst, nc.Block() as blk:
                pe_ = pst.enter_context
                wbufB = pe_(nc.sbuf_tensor("wbufB", [128, KC, D], BF16))
                wstgB = [pe_(nc.sbuf_tensor(f"wstgB{i}", [128, D], BF16)) for i in range(2)]
                ost = [pe_(nc.sbuf_tensor(f"b_o{i}", [128, KC, 128], BF16)) for i in range(2)]
                xr = [pe_(nc.sbuf_tensor(f"b_x{i}", [128, D], F32)) for i in range(2)]
                nz = pe_(nc.sbuf_tensor("b_nz", [128, D], F32))
                x2t = [pe_(nc.sbuf_tensor(f"b_2{i}", [128, D], F32)) for i in range(2)]
                bn6 = pe_(nc.sbuf_tensor("b_bn", [128, 24], F32))
                mv = pe_(nc.sbuf_tensor("b_mv", [128, 2], F32))
                vep = pe_(nc.sbuf_tensor("b_ve", [128, 1], F32))
                ngm = pe_(nc.sbuf_tensor("b_ng", [128, 1], F32))
                alp = pe_(nc.sbuf_tensor("b_al", [128, 1], F32))
                bet = pe_(nc.sbuf_tensor("b_be", [128, 1], F32))
                lnt = pe_(nc.sbuf_tensor("b_lt", [128, 1], F32))
                ps = [pe_(nc.psum_tensor(f"b_p{i}", [128, D], F32)) for i in range(2)]
                b_w1 = Sem(pst, nc, "b_w1")
                b_w2 = Sem(pst, nc, "b_w2")
                b_x = Sem(pst, nc, "b_x")
                b_pe = Sem(pst, nc, "b_pe")
                b_st = Sem(pst, nc, "b_st")
                b_nm = Sem(pst, nc, "b_nm")
                b_ad = Sem(pst, nc, "b_ad")
                b_so = Sem(pst, nc, "b_so")
                b_fv = Sem(pst, nc, "b_fv")
                b_fs = Sem(pst, nc, "b_fs")

                @blk.gpsimd
                def _(g):
                    for j in range(KC):
                        if j >= 2:
                            g.wait_ge(b_w2.h, 16 * (j - 1))
                        b_w1.inc(g.dma_start(out=wstgB[j % 2][:],
                                             in_=w_att[3][128 * j:128 * (j + 1), :]), 16)
                    for tt in range(TT_OWN):
                        b = tt % 2
                        if tt >= 2:
                            g.wait_ge(b_pe.h, 4 * (tt - 1))
                        b_x.inc(g.dma_start(out=ost[b][:], in_=o_f[:, :, 128 * tt:128 * (tt + 1)]), 16)
                        if tt >= 2:
                            g.wait_ge(b_ad.h, tt - 1)
                        b_x.inc(g.dma_start(out=xr[b][:], in_=x_til[:, tt + 2, :]), 16)

                @blk.sync
                def _(sync):
                    for j in range(KC):
                        sync.wait_ge(b_w1.h, 16 * (j + 1))
                        b_w2.inc(sync.dma_start_transpose(
                            wbufB[:, :, 128 * j:128 * (j + 1)], wstgB[j % 2][:]), 16)
                    for tt in range(TT_OWN):
                        b = tt % 2
                        sync.wait_ge(b_ad.h, tt + 1)
                        if tt >= 2:
                            sync.wait_ge(b_so.h, 16 * (tt - 1))
                        b_so.inc(sync.dma_start(out=x2m[128 * tt:128 * (tt + 1), :], in_=x2t[b][:]), 16)
                    sync.wait_ge(b_so.h, 16 * TT_OWN)

                @blk.tensor
                def _(t):
                    t.wait_ge(b_w2.h, 16 * KC)
                    for tt in range(TT_OWN):
                        b = tt % 2
                        t.wait_ge(b_x.h, 32 * tt + 16)
                        if tt >= 2:
                            t.wait_ge(b_nm.h, tt - 1)
                        for ow in range(4):
                            for kc in range(KC):
                                mm = t.matmul(ps[b][:, 512 * ow:512 * (ow + 1)],
                                              ost[b][:, kc, :],
                                              wbufB[:, kc, 512 * ow:512 * (ow + 1)],
                                              start=(kc == 0), stop=(kc == KC - 1))
                            b_pe.inc(mm, 1)

                @blk.vector
                def _(v):
                    for tt in range(TT_OWN):
                        b = tt % 2
                        v.wait_ge(b_pe.h, 4 * (tt + 1))
                        last = None
                        for q4 in range(4):
                            last = v.bn_stats(bn6[:, 6 * q4:6 * (q4 + 1)],
                                              ps[b][:, 512 * q4:512 * (q4 + 1)])
                        b_fv.inc(last, 1); v.wait_ge(b_fv.h, b_fv.n)
                        b_fv.inc(v.bn_aggr(mv[:], bn6[:]), 1); v.wait_ge(b_fv.h, b_fv.n)
                        v.tensor_scalar(ngm[:], mv[:, 0:1], -1.0, None, ALU.mult)
                        b_st.inc(v.tensor_scalar(vep[:], mv[:, 1:2], 1e-5, None, ALU.add), 1)
                        v.wait_ge(b_nm.h, tt + 1)
                        v.wait_ge(b_x.h, 32 * (tt + 1))
                        if tt >= 2:
                            v.wait_ge(b_so.h, 16 * (tt - 1))
                        b_ad.inc(v.tensor_tensor(x2t[b][:], nz[:], xr[b][:], ALU.add), 1)

                @blk.scalar
                def _(sc):
                    for tt in range(TT_OWN):
                        b = tt % 2
                        sc.wait_ge(b_st.h, tt + 1)
                        b_fs.inc(sc.activation(lnt[:], vep[:], AF.Ln, bias=0.0, scale=1.0), 1)
                        sc.wait_ge(b_fs.h, b_fs.n)
                        b_fs.inc(sc.activation(alp[:], lnt[:], AF.Exp, bias=0.0, scale=-0.5), 1)
                        sc.wait_ge(b_fs.h, b_fs.n)
                        b_fs.inc(sc.activation(bet[:], ngm[:], AF.Identity, bias=0.0, scale=alp[:]), 1)
                        sc.wait_ge(b_fs.h, b_fs.n)
                        if tt >= 1:
                            sc.wait_ge(b_ad.h, tt)
                        b_nm.inc(sc.activation(nz[:], ps[b][:], AF.Identity,
                                               bias=bet[:], scale=alp[:]), 1)

        if phases >= 8:
            with ExitStack() as pst, nc.Block() as blk:
                pe_ = pst.enter_context
                a2f = pe_(nc.sbuf_tensor("f_a2", [128, KC, 512], BF16))
                h_sb = pe_(nc.sbuf_tensor("f_h", [128, IC, 512], BF16))
                wg_g = [pe_(nc.sbuf_tensor(f"f_wg{i}", [128, KC, 128], BF16)) for i in range(3)]
                wg_u = [pe_(nc.sbuf_tensor(f"f_wu{i}", [128, KC, 128], BF16)) for i in range(3)]
                wdl = [pe_(nc.sbuf_tensor(f"f_wd{i}", [128, IC, 128], BF16)) for i in range(3)]
                sg = [pe_(nc.sbuf_tensor(f"f_sg{i}", [128, 512], BF16)) for i in range(2)]
                y_fs = [pe_(nc.sbuf_tensor(f"f_yf{i}", [128, 512], BF16)) for i in range(2)]
                y_tm = pe_(nc.sbuf_tensor("f_yt", [128, 4, D], BF16))
                x2i = [pe_(nc.sbuf_tensor(f"f_x2{i}", [128, D], F32)) for i in range(2)]
                x2r = [pe_(nc.sbuf_tensor(f"f_xr{i}", [128, D], F32)) for i in range(2)]
                a2t = [pe_(nc.sbuf_tensor(f"f_a2t{i}", [128, D], BF16)) for i in range(2)]
                ny = pe_(nc.sbuf_tensor("f_ny", [128, D], F32))
                x3 = [pe_(nc.sbuf_tensor(f"f_x3{i}", [128, D], F32)) for i in range(2)]
                bn6 = pe_(nc.sbuf_tensor("f_bn", [128, 24], F32))
                mv = pe_(nc.sbuf_tensor("f_mv", [128, 2], F32))
                vep = pe_(nc.sbuf_tensor("f_ve", [128, 1], F32))
                ngm = pe_(nc.sbuf_tensor("f_ng", [128, 1], F32))
                alp = pe_(nc.sbuf_tensor("f_al", [128, 1], F32))
                bet = pe_(nc.sbuf_tensor("f_be", [128, 1], F32))
                lnt = pe_(nc.sbuf_tensor("f_lt", [128, 1], F32))
                bn62 = pe_(nc.sbuf_tensor("f_bn2", [128, 24], F32))
                mv2 = pe_(nc.sbuf_tensor("f_mv2", [128, 2], F32))
                ngm2 = pe_(nc.sbuf_tensor("f_ng2", [128, 1], F32))
                alp2 = pe_(nc.sbuf_tensor("f_al2", [128, 1], F32))
                bet2 = pe_(nc.sbuf_tensor("f_be2", [128, 1], F32))
                lnt2 = pe_(nc.sbuf_tensor("f_lt2", [128, 1], F32))
                u1 = pe_(nc.sbuf_tensor("f_u1", [128, 1], F32))
                u2 = pe_(nc.sbuf_tensor("f_u2", [128, 1], F32))
                ps_g = [pe_(nc.psum_tensor(f"f_pg{i}", [128, 512], F32)) for i in range(2)]
                ps_u = [pe_(nc.psum_tensor(f"f_pu{i}", [128, 512], F32)) for i in range(2)]
                ps_y = [pe_(nc.psum_tensor(f"f_py{i}", [128, 512], F32)) for i in range(2)]

                f_x2 = Sem(pst, nc, "f_x2")
                f_st = Sem(pst, nc, "f_st")
                f_nm = Sem(pst, nc, "f_nm")
                f_ax = Sem(pst, nc, "f_ax")
                f_wg = Sem(pst, nc, "f_wg")
                f_wd = Sem(pst, nc, "f_wd")
                f_pe = Sem(pst, nc, "f_pe")
                f_si = Sem(pst, nc, "f_si")
                f_h = Sem(pst, nc, "f_h")
                f_yp = Sem(pst, nc, "f_yp")
                f_ye = Sem(pst, nc, "f_ye")
                f_yx = Sem(pst, nc, "f_yx")
                f_fs = Sem(pst, nc, "f_fs")
                f_fn = Sem(pst, nc, "f_fn")
                f_fa = Sem(pst, nc, "f_fa")
                f_xr = Sem(pst, nc, "f_xr")
                f_fv = Sem(pst, nc, "f_fv")
                f_fc = Sem(pst, nc, "f_fc")

                NT = nt

                @blk.gpsimd
                def _(g):
                    def wg_load(gp, p):
                        if gp >= 3:
                            g.wait_ge(f_pe.h, 2 * (gp - 2))
                        f_wg.inc(g.dma_start(out=wg_g[gp % 3][:], in_=wgu_t[p]), 16)
                        f_wg.inc(g.dma_start(out=wg_u[gp % 3][:], in_=wgu_t[p + IC]), 16)

                    for ti in range(NT):
                        lo = 3 if ti > 0 else 0
                        for p in range(lo, IC):
                            wg_load(IC * ti + p, p)
                        if ti + 1 < NT:
                            for p in range(3):
                                wg_load(IC * (ti + 1) + p, p)
                        for hc in range(HOC):
                            gh = HOC * ti + hc
                            if gh >= 3:
                                g.wait_ge(f_yp.h, gh - 2)
                            f_wd.inc(g.dma_start(out=wdl[gh % 3][:], in_=wd_t[hc]), 16)

                @blk.scalar
                def _(sc):
                    def ln2_group(tj):
                        for stt in range(4):
                            u = 4 * tj + stt
                            if u >= 2:
                                sc.wait_ge(f_nm.h, u - 1)
                            f_x2.inc(sc.dma_start(
                                out=x2i[u % 2][:],
                                in_=x2m[512 * tj + 128 * stt:512 * tj + 128 * (stt + 1), :]), 16)
                            sc.wait_ge(f_st.h, u + 1)
                            f_fc.inc(sc.activation(lnt[:], vep[:], AF.Ln, bias=0.0, scale=1.0), 1)
                            sc.wait_ge(f_fc.h, f_fc.n)
                            f_fc.inc(sc.activation(alp[:], lnt[:], AF.Exp, bias=0.0, scale=-0.5), 1)
                            sc.wait_ge(f_fc.h, f_fc.n)
                            f_fc.inc(sc.activation(bet[:], ngm[:], AF.Identity, bias=0.0, scale=alp[:]), 1)
                            sc.wait_ge(f_fc.h, f_fc.n)
                            if u >= 2:
                                sc.wait_ge(f_ax.h, 16 * (u - 1))
                            a2tv = a2t[u % 2][:].rearrange("t (j p) -> t j p", j=4)
                            x2iv = x2i[u % 2][:].rearrange("t (p j) -> t j p", j=4)
                            f_nm.inc(sc.activation(a2tv, x2iv, AF.Identity,
                                                   bias=bet[:], scale=alp[:]), 1)

                    ln2_group(0)
                    for ti in range(NT):
                        for p in range(IC):
                            gp = IC * ti + p
                            sc.wait_ge(f_pe.h, 2 * gp + 1)
                            if gp >= 2:
                                sc.wait_ge(f_h.h, gp - 1)
                            f_si.inc(sc.activation(sg[gp % 2][:], ps_g[gp % 2][:], AF.Silu), 1)
                        if ti + 1 < NT:
                            ln2_group(ti + 1)
                        for stt in range(4):
                            u = 4 * ti + stt
                            sc.wait_ge(f_fs.h, u + 1)
                            f_fc.inc(sc.activation(lnt2[:], u2[:], AF.Ln, bias=0.0, scale=1.0), 1)
                            sc.wait_ge(f_fc.h, f_fc.n)
                            f_fc.inc(sc.activation(alp2[:], lnt2[:], AF.Exp, bias=0.0, scale=-0.5), 1)
                            sc.wait_ge(f_fc.h, f_fc.n)
                            f_fc.inc(sc.activation(bet2[:], ngm2[:], AF.Identity, bias=0.0, scale=alp2[:]), 1)
                            sc.wait_ge(f_fc.h, f_fc.n)
                            if u >= 1:
                                sc.wait_ge(f_fa.h, u)
                            f_fn.inc(sc.activation(ny[:], y_tm[:, stt, :], AF.Identity,
                                                   bias=bet2[:], scale=alp2[:]), 1)

                @blk.vector
                def _(v):
                    def stats_group(tj):
                        for stt in range(4):
                            u = 4 * tj + stt
                            v.wait_ge(f_x2.h, 16 * (u + 1))
                            last = None
                            for q4 in range(4):
                                last = v.bn_stats(bn6[:, 6 * q4:6 * (q4 + 1)],
                                                  x2i[u % 2][:, 512 * q4:512 * (q4 + 1)])
                            f_fv.inc(last, 1); v.wait_ge(f_fv.h, f_fv.n)
                            f_fv.inc(v.bn_aggr(mv[:], bn6[:]), 1); v.wait_ge(f_fv.h, f_fv.n)
                            v.tensor_scalar(ngm[:], mv[:, 0:1], -1.0, None, ALU.mult)
                            f_st.inc(v.tensor_scalar(vep[:], mv[:, 1:2], 1e-5, None, ALU.add), 1)

                    stats_group(0)
                    for ti in range(NT):
                        for p in range(IC):
                            gp = IC * ti + p
                            if ti >= 1 and p == 0:
                                v.wait_ge(f_yp.h, HOC * ti)
                            v.wait_ge(f_si.h, gp + 1)
                            v.wait_ge(f_pe.h, 2 * gp + 2)
                            f_h.inc(v.tensor_tensor(h_sb[:, p, :], sg[gp % 2][:],
                                                    ps_u[gp % 2][:], ALU.mult), 1)
                        if ti + 1 < NT:
                            stats_group(ti + 1)
                        for hc in range(HOC):
                            gh = HOC * ti + hc
                            v.wait_ge(f_yp.h, gh + 1)
                            if gh >= 2:
                                v.wait_ge(f_yx.h, 16 * (gh - 1))
                            f_ye.inc(v.tensor_copy(y_fs[gh % 2][:], ps_y[gh % 2][:]), 1)
                        for stt in range(4):
                            u = 4 * ti + stt
                            v.wait_ge(f_yx.h, 16 * HOC * (ti + 1))
                            last = None
                            for q4 in range(4):
                                last = v.bn_stats(bn62[:, 6 * q4:6 * (q4 + 1)],
                                                  y_tm[:, stt, 512 * q4:512 * (q4 + 1)])
                            f_fv.inc(last, 1); v.wait_ge(f_fv.h, f_fv.n)
                            f_fv.inc(v.bn_aggr(mv2[:], bn62[:]), 1); v.wait_ge(f_fv.h, f_fv.n)
                            v.tensor_scalar(ngm2[:], mv2[:, 0:1], -1.0, None, ALU.mult)
                            f_fv.inc(v.tensor_scalar(u1[:], mv2[:, 0:1], mv2[:, 0:1], mv2[:, 1:2],
                                                     ALU.mult, ALU.add), 1)
                            v.wait_ge(f_fv.h, f_fv.n)
                            f_fv.inc(v.tensor_scalar(u2[:], u1[:], 1e-5, mv2[:, 1:2],
                                                     ALU.mult, ALU.add), 1)
                            v.wait_ge(f_fv.h, f_fv.n)
                            f_fs.inc(v.tensor_scalar(u2[:], u2[:], 1e-11, None, ALU.add), 1)
                            v.wait_ge(f_fn.h, u + 1)
                            v.wait_ge(f_xr.h, 16 * (u + 1))
                            if u >= 2:
                                v.wait_ge(s_fin.h, 16 * (u - 1))
                            f_fa.inc(v.tensor_tensor(x3[u % 2][:], ny[:], x2r[u % 2][:], ALU.add), 1)

                @blk.sync
                def _(sync):
                    def a2_xbars(tj):
                        for stt in range(4):
                            u = 4 * tj + stt
                            sync.wait_ge(f_nm.h, u + 1)
                            if tj >= 1 and stt == 0:
                                sync.wait_ge(f_pe.h, 2 * IC * tj)
                            f_ax.inc(sync.dma_start_transpose(
                                a2f[:, :, 128 * stt:128 * (stt + 1)], a2t[u % 2][:]), 16)

                    a2_xbars(0)
                    for ti in range(NT):
                        if ti + 1 < NT:
                            a2_xbars(ti + 1)
                        for hc in range(HOC):
                            gh = HOC * ti + hc
                            sync.wait_ge(f_ye.h, gh + 1)
                            if ti >= 1 and hc == 0:
                                sync.wait_ge(f_fn.h, 4 * ti)
                            f_yx.inc(sync.dma_start_transpose(
                                y_tm[:, :, 128 * hc:128 * (hc + 1)], y_fs[gh % 2][:]), 16)
                        for stt in range(4):
                            u = 4 * ti + stt
                            if u >= 2:
                                sync.wait_ge(f_fa.h, u - 1)
                            f_xr.inc(sync.dma_start(
                                out=x2r[u % 2][:],
                                in_=x2m[512 * ti + 128 * stt:512 * ti + 128 * (stt + 1), :]), 16)
                        for stt in range(4):
                            u = 4 * ti + stt
                            sync.wait_ge(f_fa.h, u + 1)
                            s_fin.inc(sync.dma_start(
                                out=out[512 * ti + 128 * stt:512 * ti + 128 * (stt + 1), :],
                                in_=x3[u % 2][:]), 16)
                    sync.wait_ge(s_fin.h, 16 * 4 * NT)

                @blk.tensor
                def _(t):
                    for ti in range(NT):
                        t.wait_ge(f_ax.h, 16 * 4 * (ti + 1))
                        for p in range(IC):
                            gp = IC * ti + p
                            t.wait_ge(f_wg.h, 32 * (gp + 1))
                            if gp >= 2:
                                t.wait_ge(f_si.h, gp - 1)
                                t.wait_ge(f_h.h, gp - 1)
                            for kc in range(KC):
                                mm = t.matmul(ps_g[gp % 2][:], wg_g[gp % 3][:, kc, :],
                                              a2f[:, kc, :], start=(kc == 0), stop=(kc == KC - 1))
                            f_pe.inc(mm, 1)
                            for kc in range(KC):
                                mm = t.matmul(ps_u[gp % 2][:], wg_u[gp % 3][:, kc, :],
                                              a2f[:, kc, :], start=(kc == 0), stop=(kc == KC - 1))
                            f_pe.inc(mm, 1)
                        for hc in range(HOC):
                            gh = HOC * ti + hc
                            t.wait_ge(f_wd.h, 16 * (gh + 1))
                            t.wait_ge(f_h.h, IC * (ti + 1))
                            if gh >= 2:
                                t.wait_ge(f_ye.h, gh - 1)
                            for ic in range(IC):
                                mm = t.matmul(ps_y[gh % 2][:], wdl[gh % 3][:, ic, :],
                                              h_sb[:, ic, :], start=(ic == 0), stop=(ic == IC - 1))
                            f_yp.inc(mm, 1)

        st.close()
        return nc
    return nc


def _host_prep(L0_attn_x, Wq, Wk, Wv, Wo, gup_a, gus_a, dnp_a, dns_a):
    x = np.asarray(L0_attn_x, dtype=np.float32).reshape(16384, 2048)
    inv = 1.0 / (10000.0 ** (np.arange(0, 128, 2, dtype=np.float64) / 128.0))
    dns_pad = np.zeros((48, 2048), dtype=np.float16)
    dns_pad[0:44] = np.asarray(dns_a, dtype=np.float16)
    scale = 1.0 / np.sqrt(128.0)
    Wq = np.ascontiguousarray(np.asarray(Wq, np.float32))
    Wk = np.ascontiguousarray(np.asarray(Wk, np.float32))
    Wv = np.ascontiguousarray(np.asarray(Wv, np.float32))
    Wo = np.ascontiguousarray(np.asarray(Wo, np.float32))
    gup_a = np.ascontiguousarray(np.asarray(gup_a, np.int32)
                                 .reshape(2, 1408, 4, 256).transpose(0, 2, 1, 3)
                                 .reshape(11264, 256))
    gus_a = np.ascontiguousarray(np.asarray(gus_a, np.float16)
                                 .reshape(16, 2, 1408, 4).transpose(0, 1, 3, 2)
                                 .reshape(16, 11264))
    dnp_a = np.ascontiguousarray(np.asarray(dnp_a, np.int32))
    tril = np.tril(np.ones((256, 256), np.float32))
    base = np.zeros((256, 512), np.float32)
    base[:, 0:256] = 1.0
    base[:, 256:512] = tril
    first = np.zeros((256, 512), np.float32)
    first[:, 256:512] = tril
    mB_ = np.ascontiguousarray(base.reshape(2, 128, 512).transpose(1, 0, 2))
    mF_ = np.ascontiguousarray(first.reshape(2, 128, 512).transpose(1, 0, 2))
    in_maps = []
    for c in range(NCORES):
        lo = c * T_OWN
        halo = np.zeros((256, 2048), np.float32) if c == 0 else x[lo - 256:lo]
        x_slice = np.ascontiguousarray(np.concatenate([halo, x[lo:lo + T_OWN]], axis=0))
        pos_all = np.arange(lo - 256, lo + T_OWN, dtype=np.float64)
        if c == 0:
            pos_all = np.maximum(pos_all, 0.0)
        ang = np.outer(pos_all, inv)
        ck_t = np.ascontiguousarray(np.cos(ang).astype(np.float32).reshape(TT_ALL, 128, 64).transpose(1, 0, 2))
        sk_t = np.ascontiguousarray(np.sin(ang).astype(np.float32).reshape(TT_ALL, 128, 64).transpose(1, 0, 2))
        ang_o = ang[256:]
        cq_t = np.ascontiguousarray((np.cos(ang_o) * scale).astype(np.float32).reshape(TT_OWN, 128, 64).transpose(1, 0, 2))
        sq_t = np.ascontiguousarray((np.sin(ang_o) * scale).astype(np.float32).reshape(TT_OWN, 128, 64).transpose(1, 0, 2))
        in_maps.append({
            "x": x_slice, "wq": Wq, "wk": Wk, "wv": Wv, "wo": Wo,
            "gup": gup_a, "gus": gus_a, "dnp": dnp_a, "dns": dns_pad,
            "cq": cq_t, "sq": sq_t, "ck": ck_t, "sk": sk_t,
            "m0": (mF_ if c == 0 else mB_), "mB": mB_,
        })
    return in_maps


def kernel(L0_attn_x, L0_attn_Wq, L0_attn_Wk, L0_attn_Wv, L0_attn_Wo,
           L0_ffn_gate_up_packed, L0_ffn_gate_up_scales,
           L0_ffn_down_packed, L0_ffn_down_scales):
    in_maps = _host_prep(L0_attn_x, L0_attn_Wq, L0_attn_Wk, L0_attn_Wv, L0_attn_Wo,
                         L0_ffn_gate_up_packed, L0_ffn_gate_up_scales,
                         L0_ffn_down_packed, L0_ffn_down_scales)
    nc = build()
    res = run_bass_kernel_spmd(nc, in_maps, core_ids=list(range(NCORES)))
    out = np.concatenate([r["out"] for r in res.results], axis=0)
    return out.reshape(1, 16384, 2048)


# version 14
# speedup vs baseline: 1.0269x; 1.0269x over previous
import numpy as np
from contextlib import ExitStack

import concourse.bass as bass
import concourse.mybir as mybir
from concourse.bass_utils import run_bass_kernel_spmd

F32 = mybir.dt.float32
BF16 = mybir.dt.bfloat16
FP16 = mybir.dt.float16
U32 = mybir.dt.uint32
U16 = mybir.dt.uint16
I32 = mybir.dt.int32
AF = mybir.ActivationFunctionType
ALU = mybir.AluOpType
AX = mybir.AxisListType

NCORES = 8
D = 2048
KC = 16
NH = 16
W = 256
T_OWN = 2048
T_ALL = 2304
TT_OWN = 16
TT_ALL = 18
NCHUNK = 8
I_FFN = 5632
IC = 44
GU = 11264
GOC = 88
HOC = 16

SIGN_MASK = 0x80008000 - (1 << 32)
EEM_MASK = 0x0E000E00


class Sem:
    def __init__(self, st, nc, name):
        self.h = st.enter_context(nc.semaphore(name))
        self.n = 0

    def inc(self, call, k=1):
        call.then_inc(self.h, k)
        self.n += k
        return self.n


def build(phases=9, debug=(), nt=4):
    nc = bass.Bass("TRN2", target_bir_lowering=False)

    x_in = nc.dram_tensor("x", [T_ALL, D], F32, kind="ExternalInput").ap()
    w_att = [nc.dram_tensor(n, [D, D], F32, kind="ExternalInput").ap()
             for n in ("wq", "wk", "wv", "wo")]
    gup = nc.dram_tensor("gup", [GU, 256], I32, kind="ExternalInput").ap()
    gus = nc.dram_tensor("gus", [16, GU], FP16, kind="ExternalInput").ap()
    dnp = nc.dram_tensor("dnp", [D, 704], I32, kind="ExternalInput").ap()
    dns = nc.dram_tensor("dns", [48, D], FP16, kind="ExternalInput").ap()
    cq_in = nc.dram_tensor("cq", [128, TT_OWN, 64], F32, kind="ExternalInput").ap()
    sq_in = nc.dram_tensor("sq", [128, TT_OWN, 64], F32, kind="ExternalInput").ap()
    ck_in = nc.dram_tensor("ck", [128, TT_ALL, 64], F32, kind="ExternalInput").ap()
    sk_in = nc.dram_tensor("sk", [128, TT_ALL, 64], F32, kind="ExternalInput").ap()
    m0_in = nc.dram_tensor("m0", [128, 2, 512], F32, kind="ExternalInput").ap()
    mB_in = nc.dram_tensor("mB", [128, 2, 512], F32, kind="ExternalInput").ap()

    out = nc.dram_tensor("out", [T_OWN, D], F32, kind="ExternalOutput").ap()

    def scratch(name, shape, dt):
        kind = "ExternalOutput" if (name in debug) else "Internal"
        return nc.dram_tensor(name, shape, dt, kind=kind).ap()

    a1f = scratch("a1f", [128, TT_ALL, KC, 128], BF16)
    qtm = scratch("qtm", [T_OWN, D], BF16)
    ktm = scratch("ktm", [T_ALL, D], BF16)
    vtm = scratch("vtm", [T_ALL, D], BF16)
    o_f = scratch("o_f", [128, KC, T_OWN], BF16)
    x2m = scratch("x2m", [T_OWN, D], F32)
    wgu_t = scratch("wgu_t", [GOC, 128, KC, 128], BF16)
    wd_t = scratch("wd_t", [HOC, 128, IC, 128], BF16)

    x_til = x_in.rearrange("(tt p) f -> p tt f", p=128)

    st = ExitStack()
    with st:
        e = st.enter_context
        cst = ExitStack()
        ce = cst.enter_context
        cq = ce(nc.sbuf_tensor("cq_sb", [128, TT_OWN, 64], F32))
        sq = ce(nc.sbuf_tensor("sq_sb", [128, TT_OWN, 64], F32))
        ck = ce(nc.sbuf_tensor("ck_sb", [128, TT_ALL, 64], F32))
        sk = ce(nc.sbuf_tensor("sk_sb", [128, TT_ALL, 64], F32))
        m0 = ce(nc.sbuf_tensor("m0_sb", [128, 2, 512], BF16))
        mB = ce(nc.sbuf_tensor("mB_sb", [128, 2, 512], BF16))
        s_gu = ce(nc.sbuf_tensor("sgu_sb", [128, GOC, 16], FP16))
        s_dn = ce(nc.sbuf_tensor("sdn_sb", [128, HOC, 48], FP16))

        s_fin = Sem(st, nc, "s_fin")

        with ExitStack() as pst, nc.Block() as blk:
            pe_ = pst.enter_context
            m0f = pe_(nc.sbuf_tensor("m0f", [128, 2, 512], F32))
            mBf = pe_(nc.sbuf_tensor("mBf", [128, 2, 512], F32))
            gus_sb = pe_(nc.sbuf_tensor("gus_sb", [16, GU], FP16))
            dns_sb = pe_(nc.sbuf_tensor("dns_sb", [48, D], FP16))
            sc_in = Sem(pst, nc, "c_in")
            sc_xb = Sem(pst, nc, "c_xb")

            @blk.sync
            def _(sync):
                for dst, src in ((cq, cq_in), (sq, sq_in), (ck, ck_in), (sk, sk_in),
                                 (m0f, m0_in), (mBf, mB_in), (gus_sb, gus), (dns_sb, dns)):
                    sc_in.inc(sync.dma_start(out=dst[:], in_=src), 16)
                sync.wait_ge(sc_in.h, sc_in.n)
                sc_xb.inc(sync.dma_start_transpose(s_gu[:], gus_sb[:]), 16)
                sc_xb.inc(sync.dma_start_transpose(s_dn[:], dns_sb[:]), 16)
                sync.wait_ge(sc_xb.h, sc_xb.n)

            @blk.vector
            def _(v):
                v.wait_ge(sc_in.h, 128)
                v.tensor_copy(m0[:], m0f[:])
                v.tensor_copy(mB[:], mBf[:])

        mid = ExitStack()
        with mid:
            wbuf = [mid.enter_context(nc.sbuf_tensor(f"wbuf{i}", [128, KC, D], BF16))
                    for i in range(2)]
            wstg = [mid.enter_context(nc.sbuf_tensor(f"wstg{i}", [128, D], BF16))
                    for i in range(2)]
            s_wst = Sem(st, nc, "s_wst")
            s_wxb = Sem(st, nc, "s_wxb")

            def emit_wprep_g(g, widx, j):
                u = widx * KC + j
                if u >= 2:
                    g.wait_ge(s_wxb.h, 16 * (u - 1))
                s_wst.inc(g.dma_start(out=wstg[j % 2][:],
                                      in_=w_att[widx][128 * j:128 * (j + 1), :]), 16)

            def emit_wprep_x(sync, widx, buf, j):
                u = widx * KC + j
                sync.wait_ge(s_wst.h, 16 * (u + 1))
                s_wxb.inc(sync.dma_start_transpose(
                    wbuf[buf][:, :, 128 * j:128 * (j + 1)], wstg[j % 2][:]), 16)

            if phases >= 1:
                with ExitStack() as pst, nc.Block() as blk:
                    pe_ = pst.enter_context
                    xin = [pe_(nc.sbuf_tensor(f"p0x{i}", [128, D], F32)) for i in range(2)]
                    a1t = [pe_(nc.sbuf_tensor(f"p0a{i}", [128, D], BF16)) for i in range(2)]
                    xbo = [pe_(nc.sbuf_tensor(f"p0b{i}", [128, KC, 128], BF16)) for i in range(2)]
                    bn6 = [pe_(nc.sbuf_tensor(f"p0n{i}", [128, 24], F32)) for i in range(2)]
                    mv = [pe_(nc.sbuf_tensor(f"p0m{i}", [128, 2], F32)) for i in range(2)]
                    vep = [pe_(nc.sbuf_tensor(f"p0v{i}", [128, 1], F32)) for i in range(2)]
                    ngm = [pe_(nc.sbuf_tensor(f"p0g{i}", [128, 1], F32)) for i in range(2)]
                    alp = [pe_(nc.sbuf_tensor(f"p0A{i}", [128, 1], F32)) for i in range(2)]
                    bet = [pe_(nc.sbuf_tensor(f"p0B{i}", [128, 1], F32)) for i in range(2)]
                    lnt = [pe_(nc.sbuf_tensor(f"p0L{i}", [128, 1], F32)) for i in range(2)]
                    p_x = Sem(pst, nc, "p0_x")
                    p_fv = Sem(pst, nc, "p0_fv")
                    p_fs = Sem(pst, nc, "p0_fs")
                    p_st = Sem(pst, nc, "p0_st")
                    p_nm = Sem(pst, nc, "p0_nm")
                    p_xb = Sem(pst, nc, "p0_xb")
                    p_so = Sem(pst, nc, "p0_so")

                    @blk.gpsimd
                    def _(g):
                        for tt in range(TT_ALL):
                            b = tt % 2
                            if tt >= 2:
                                g.wait_ge(p_nm.h, tt - 1)
                            p_x.inc(g.dma_start(out=xin[b][:], in_=x_til[:, tt, :]), 16)
                            if tt < KC:
                                emit_wprep_g(g, 0, tt)

                    @blk.vector
                    def _(v):
                        for tt in range(TT_ALL):
                            b = tt % 2
                            v.wait_ge(p_x.h, 16 * (tt + 1))
                            last = None
                            for q4 in range(4):
                                last = v.bn_stats(bn6[b][:, 6 * q4:6 * (q4 + 1)],
                                                  xin[b][:, 512 * q4:512 * (q4 + 1)])
                            p_fv.inc(last, 1); v.wait_ge(p_fv.h, p_fv.n)
                            p_fv.inc(v.bn_aggr(mv[b][:], bn6[b][:]), 1); v.wait_ge(p_fv.h, p_fv.n)
                            v.tensor_scalar(ngm[b][:], mv[b][:, 0:1], -1.0, None, ALU.mult)
                            p_st.inc(v.tensor_scalar(vep[b][:], mv[b][:, 1:2], 1e-5, None, ALU.add), 1)

                    @blk.scalar
                    def _(sc):
                        for tt in range(TT_ALL):
                            b = tt % 2
                            sc.wait_ge(p_st.h, tt + 1)
                            p_fs.inc(sc.activation(lnt[b][:], vep[b][:], AF.Ln, bias=0.0, scale=1.0), 1)
                            sc.wait_ge(p_fs.h, p_fs.n)
                            p_fs.inc(sc.activation(alp[b][:], lnt[b][:], AF.Exp, bias=0.0, scale=-0.5), 1)
                            sc.wait_ge(p_fs.h, p_fs.n)
                            p_fs.inc(sc.activation(bet[b][:], ngm[b][:], AF.Identity, bias=0.0, scale=alp[b][:]), 1)
                            sc.wait_ge(p_fs.h, p_fs.n)
                            if tt >= 2:
                                sc.wait_ge(p_xb.h, 16 * (tt - 1))
                            p_nm.inc(sc.activation(a1t[b][:], xin[b][:], AF.Identity,
                                                   bias=bet[b][:], scale=alp[b][:]), 1)

                    @blk.sync
                    def _(sync):
                        for tt in range(TT_ALL):
                            b = tt % 2
                            sync.wait_ge(p_nm.h, tt + 1)
                            if tt >= 2:
                                sync.wait_ge(p_so.h, 16 * (tt - 1))
                            p_xb.inc(sync.dma_start_transpose(xbo[b][:], a1t[b][:]), 16)
                            sync.wait_ge(p_xb.h, 16 * (tt + 1))
                            p_so.inc(sync.dma_start(out=a1f[:, tt], in_=xbo[b][:]), 16)
                            if tt < KC:
                                emit_wprep_x(sync, 0, 0, tt)
                        sync.wait_ge(p_so.h, 16 * TT_ALL)

            def proj_phase(pid, wb, prep_widx, prep_buf, tts, rope, out_ap, tt0, wxb_need=0):
                with ExitStack() as pst, nc.Block() as blk:
                    pe_ = pst.enter_context
                    a1s = [pe_(nc.sbuf_tensor(f"q{pid}a{i}", [128, KC, 128], BF16)) for i in range(2)]
                    osb = [pe_(nc.sbuf_tensor(f"q{pid}o{i}", [128, D], BF16)) for i in range(2)]
                    tmp = [pe_(nc.sbuf_tensor(f"q{pid}t{i}", [128, 4, 64], F32)) for i in range(4)]
                    ps = [pe_(nc.psum_tensor(f"q{pid}p{i}", [128, D], F32)) for i in range(2)]
                    p_x = Sem(pst, nc, f"q{pid}_x")
                    p_pe = Sem(pst, nc, f"q{pid}_pe")
                    p_ev = Sem(pst, nc, f"q{pid}_ev")
                    p_so = Sem(pst, nc, f"q{pid}_so")
                    ot = out_ap.rearrange("(tt p) f -> p tt f", p=128)
                    n = len(tts)

                    @blk.gpsimd
                    def _(g):
                        for ti in range(n):
                            b = ti % 2
                            if ti >= 2:
                                g.wait_ge(p_pe.h, 4 * (ti - 1))
                            p_x.inc(g.dma_start(out=a1s[b][:], in_=a1f[:, tts[ti]]), 16)
                            if prep_widx is not None and ti < KC:
                                emit_wprep_g(g, prep_widx, ti)

                    @blk.tensor
                    def _(t):
                        t.wait_ge(s_wxb.h, wxb_need)
                        for ti in range(n):
                            b = ti % 2
                            t.wait_ge(p_x.h, 16 * (ti + 1))
                            if ti >= 2:
                                t.wait_ge(p_ev.h, 4 * (ti - 1))
                            for ow in range(4):
                                for kc in range(KC):
                                    mm = t.matmul(ps[b][:, 512 * ow:512 * (ow + 1)],
                                                  a1s[b][:, kc, :],
                                                  wb[:, kc, 512 * ow:512 * (ow + 1)],
                                                  start=(kc == 0), stop=(kc == KC - 1))
                                p_pe.inc(mm, 1)

                    @blk.vector
                    def _(v):
                        for ti in range(n):
                            b = ti % 2
                            for ow in range(4):
                                v.wait_ge(p_pe.h, 4 * ti + ow + 1)
                                if ti >= 2:
                                    v.wait_ge(p_so.h, 16 * (ti - 1))
                                win = ps[b][:, 512 * ow:512 * (ow + 1)].rearrange(
                                    "p (h x) -> p h x", h=4)
                                owin = osb[b][:, 512 * ow:512 * (ow + 1)].rearrange(
                                    "p (h x) -> p h x", h=4)
                                if rope is not None:
                                    c_sb, s_sb, toff = rope
                                    tti = tts[ti] - toff
                                    c_b = c_sb[:, tti:tti + 1, :].broadcast_to([128, 4, 64])
                                    s_b = s_sb[:, tti:tti + 1, :].broadcast_to([128, 4, 64])
                                    ev, od = win[:, :, 0::2], win[:, :, 1::2]
                                    v.tensor_tensor(tmp[0][:], ev, c_b, ALU.mult)
                                    v.tensor_tensor(tmp[1][:], od, s_b, ALU.mult)
                                    v.tensor_tensor(owin[:, :, 0::2], tmp[0][:], tmp[1][:], ALU.subtract)
                                    v.tensor_tensor(tmp[2][:], ev, s_b, ALU.mult)
                                    v.tensor_tensor(tmp[3][:], od, c_b, ALU.mult)
                                    last = v.tensor_tensor(owin[:, :, 1::2], tmp[2][:], tmp[3][:], ALU.add)
                                else:
                                    last = v.tensor_copy(osb[b][:, 512 * ow:512 * (ow + 1)],
                                                         ps[b][:, 512 * ow:512 * (ow + 1)])
                                p_ev.inc(last, 1)

                    @blk.sync
                    def _(sync):
                        for ti in range(n):
                            b = ti % 2
                            sync.wait_ge(p_ev.h, 4 * (ti + 1))
                            p_so.inc(sync.dma_start(out=ot[:, tts[ti] - tt0, :], in_=osb[b][:]), 16)
                            if prep_widx is not None and ti < KC:
                                emit_wprep_x(sync, prep_widx, prep_buf, ti)
                        sync.wait_ge(p_so.h, 16 * n)

            if phases >= 2:
                proj_phase(0, wbuf[0], 1, 1, list(range(2, TT_ALL)), (cq, sq, 2), qtm, 2, wxb_need=16 * KC)
            if phases >= 3:
                proj_phase(1, wbuf[1], 2, 0, list(range(TT_ALL)), (ck, sk, 0), ktm, 0, wxb_need=32 * KC)
            if phases >= 4:
                proj_phase(2, wbuf[0], None, None, list(range(TT_ALL)), None, vtm, 0, wxb_need=48 * KC)


        if phases >= 5:
            with ExitStack() as pst, nc.Block() as blk:
                pe_ = pst.enter_context
                dq_w = [pe_(nc.sbuf_tensor(f"dqw{i}", [128, 704], U32)) for i in range(2)]
                dq_A = pe_(nc.sbuf_tensor("dqA", [128, 1408], U16))
                dq_B = pe_(nc.sbuf_tensor("dqB", [128, 1408], U16))
                dq_C = pe_(nc.sbuf_tensor("dqC", [128, 1408], U16))
                s_ex = [pe_(nc.sbuf_tensor(f"dqs{i}", [128, 1408], FP16)) for i in range(2)]
                dqq = [pe_(nc.sbuf_tensor(f"dqq{i}", [128, I_FFN], BF16)) for i in range(2)]
                dq_o = [pe_(nc.sbuf_tensor(f"dqo{i}", [128, IC, 128], BF16)) for i in range(2)]
                d_in = Sem(pst, nc, "d_in")
                d_v = Sem(pst, nc, "d_v")
                d_xb = Sem(pst, nc, "d_xb")
                d_so = Sem(pst, nc, "d_so")
                NTOT = GOC + HOC

                def spec(i):
                    return ("gu", i) if i < GOC else ("dn", i - GOC)

                def emit_store(g, i):
                    kind, idx = spec(i)
                    b = i % 2
                    g.wait_ge(d_xb.h, 16 * (i + 1))
                    if kind == "gu":
                        d_so.inc(g.dma_start(out=wgu_t[idx], in_=dq_o[b][:, 0:KC, :]), 16)
                    else:
                        d_so.inc(g.dma_start(out=wd_t[idx], in_=dq_o[b][:, 0:IC, :]), 16)

                @blk.gpsimd
                def _(g):
                    for i in range(NTOT):
                        kind, idx = spec(i)
                        b = i % 2
                        if i >= 2:
                            g.wait_ge(d_v.h, i - 1)
                        if kind == "gu":
                            d_in.inc(g.dma_start(
                                out=dq_w[b][:, 0:256],
                                in_=gup[128 * idx:128 * (idx + 1), :].bitcast(U32)), 16)
                        else:
                            d_in.inc(g.dma_start(out=dq_w[b][:, 0:704],
                                                 in_=dnp[128 * idx:128 * (idx + 1), :].bitcast(U32)), 16)
                        if i >= 2:
                            emit_store(g, i - 2)
                    emit_store(g, NTOT - 2)
                    emit_store(g, NTOT - 1)
                    g.wait_ge(d_so.h, 16 * NTOT)

                @blk.vector
                def _(v):
                    for i in range(NTOT):
                        kind, idx = spec(i)
                        b = i % 2
                        nw = 256 if kind == "gu" else 704
                        nq = 2 * nw
                        ng = nq // 32
                        v.wait_ge(d_in.h, 16 * (i + 1))
                        if i >= 2:
                            v.wait_ge(d_xb.h, 16 * (i - 1))
                        if kind == "gu":
                            s_src = s_gu[:, idx, :].unsqueeze(2).broadcast_to([128, ng, 32])
                        else:
                            s_src = s_dn[:, idx, 0:44].unsqueeze(2).broadcast_to([128, ng, 32])
                        v.tensor_scalar(s_ex[b][:, 0:nq].rearrange("p (g x) -> p g x", g=ng),
                                        s_src, 16384.0, None, ALU.mult)
                        wv = dq_w[b][:, 0:nw].bitcast(U16)
                        for j in range(4):
                            if j < 3:
                                v.tensor_scalar(dq_A[:, 0:nq], wv, 9 - 4 * j, 0x0E00,
                                                ALU.logical_shift_left, ALU.bitwise_and)
                                v.tensor_scalar(dq_B[:, 0:nq], wv, 12 - 4 * j, 0x8000,
                                                ALU.logical_shift_left, ALU.bitwise_and)
                            else:
                                v.tensor_scalar(dq_A[:, 0:nq], wv, 3, 0x0E00,
                                                ALU.logical_shift_right, ALU.bitwise_and)
                                v.tensor_scalar(dq_B[:, 0:nq], wv, 0x8000, None,
                                                ALU.bitwise_and)
                            v.tensor_tensor(dq_C[:, 0:nq], dq_A[:, 0:nq], dq_B[:, 0:nq],
                                            ALU.bitwise_or)
                            last = v.tensor_tensor(dqq[b][:, j * nq:(j + 1) * nq],
                                                   dq_C[:, 0:nq].bitcast(FP16),
                                                   s_ex[b][:, 0:nq], ALU.mult)
                        d_v.inc(last, 1)

                @blk.sync
                def _(sync):
                    for i in range(NTOT):
                        kind, idx = spec(i)
                        b = i % 2
                        nk = 2048 if kind == "gu" else I_FFN
                        ncc = KC if kind == "gu" else IC
                        sync.wait_ge(d_v.h, i + 1)
                        if i >= 2:
                            sync.wait_ge(d_so.h, 16 * (i - 1))
                        d_xb.inc(sync.dma_start_transpose(dq_o[b][:, 0:ncc, :], dqq[b][:, 0:nk]), 16)

        if phases >= 6:
            dbg4 = "dbg4" in debug
            if dbg4:
                pr_d = nc.dram_tensor("pr_d", [128, NH, 512], BF16, kind="ExternalOutput").ap()
                pT_d = nc.dram_tensor("pT_d", [128, 64, 128], BF16, kind="ExternalOutput").ap()
                nmx_d = nc.dram_tensor("nmx_d", [128, NH], F32, kind="ExternalOutput").ap()
                sm_d = nc.dram_tensor("sm_d", [128, NH], F32, kind="ExternalOutput").ap()
                qf_d = nc.dram_tensor("qf_d", [128, 2, KC, 128], BF16, kind="ExternalOutput").ap()
                osb_d = nc.dram_tensor("osb_d", [128, NH, 256], BF16, kind="ExternalOutput").ap()
                kf_d = nc.dram_tensor("kf_d", [128, 4, KC, 128], BF16, kind="ExternalOutput").ap()
                vf_d = nc.dram_tensor("vf_d", [128, 4, D], BF16, kind="ExternalOutput").ap()
            with ExitStack() as pst, nc.Block() as blk:
                pe_ = pst.enter_context
                qf = pe_(nc.sbuf_tensor("a_q", [128, 2, KC, 128], BF16))
                kf = pe_(nc.sbuf_tensor("a_k", [128, 4, KC, 128], BF16))
                vf = pe_(nc.sbuf_tensor("a_v", [128, 4, D], BF16))
                pr = [pe_(nc.sbuf_tensor(f"a_p{i}", [128, NH, 512], BF16)) for i in range(2)]
                pr2 = pe_(nc.sbuf_tensor("a_p2", [128, NH, 512], BF16))
                pT = pe_(nc.sbuf_tensor("a_T", [128, 64, 128], BF16))
                osb = [pe_(nc.sbuf_tensor(f"a_o{i}", [128, NH, 256], BF16)) for i in range(2)]
                nmx = pe_(nc.sbuf_tensor("a_nm", [128, NH], F32))
                sm = pe_(nc.sbuf_tensor("a_sm", [128, NH], F32))
                rs = pe_(nc.sbuf_tensor("a_rs", [128, NH], F32))
                dmy = pe_(nc.sbuf_tensor("a_dmy", [128, 1], F32))
                smsnap = pe_(nc.sbuf_tensor("a_smsnap", [128, NH], F32))
                ps_s = pe_(nc.psum_tensor("a_ps", [128, 2048], F32))
                ps_a = [pe_(nc.psum_tensor(f"a_pa{i}", [128, 512], F32)) for i in range(2)]
                a_ld = Sem(pst, nc, "a_ld")
                a_vl = Sem(pst, nc, "a_vl")
                a_sc = Sem(pst, nc, "a_sc")
                a_nm = Sem(pst, nc, "a_nm")
                a_ex = Sem(pst, nc, "a_ex")
                a_st = Sem(pst, nc, "a_st")
                a_nr = Sem(pst, nc, "a_nr")
                a_px = Sem(pst, nc, "a_px")
                a_av = Sem(pst, nc, "a_av")
                a_ev = Sem(pst, nc, "a_ev")
                a_os = Sem(pst, nc, "a_os")
                a_fv = Sem(pst, nc, "a_fv")
                a_dbg = Sem(pst, nc, "a_dbg")
                GH = 8

                @blk.sync
                def _(sync):
                    def pt_xbar(hp):
                        sync.wait_ge(a_nr.h, hp + 1)
                        if hp >= 1:
                            sync.wait_ge(a_av.h, 4 * hp)
                        a_px.inc(sync.dma_start_transpose(
                            pT[:], pr[hp % 2][:].rearrange("p h x -> p (h x)")), 16)

                    for c in range(NCHUNK):
                        for hp in (2 * c - 2, 2 * c - 1):
                            if hp >= 0:
                                pt_xbar(hp)
                        if c >= 1:
                            sync.wait_ge(a_sc.h, 2 * GH * c)
                        for hf in range(2):
                            a_ld.inc(sync.dma_start_transpose(
                                qf[:, hf], qtm[256 * c + 128 * hf:256 * c + 128 * (hf + 1), :]), 16)
                        for hh in range(4):
                            a_ld.inc(sync.dma_start_transpose(
                                kf[:, hh], ktm[256 * c + 128 * hh:256 * c + 128 * (hh + 1), :]), 16)
                        for half in range(2):
                            hp = 2 * c + half
                            if dbg4 and False and c == 2 and half == 0:
                                sync.wait_ge(a_px.h, 16)
                                a_dbg.inc(sync.dma_start(out=pr_d.rearrange("p a b -> p (a b)"),
                                               in_=pr[0][:].rearrange("p a b -> p (a b)")), 16)
                                a_dbg.inc(sync.dma_start(out=pT_d.rearrange("p a b -> p (a b)"),
                                               in_=pT[:].rearrange("p a b -> p (a b)")), 16)
                                a_dbg.inc(sync.dma_start(out=nmx_d, in_=nmx[:]), 16)
                                a_dbg.inc(sync.dma_start(out=sm_d, in_=smsnap[:]), 16)
                                a_dbg.inc(sync.dma_start(out=qf_d.rearrange("p a b c -> p (a b c)"),
                                               in_=qf[:].rearrange("p a b c -> p (a b c)")), 16)
                                a_dbg.inc(sync.dma_start(out=kf_d.rearrange("p a b c -> p (a b c)"),
                                               in_=kf[:].rearrange("p a b c -> p (a b c)")), 16)
                                a_dbg.inc(sync.dma_start(out=vf_d.rearrange("p a b -> p (a b)"),
                                               in_=vf[:].rearrange("p a b -> p (a b)")), 16)
                                sync.wait_ge(a_dbg.h, a_dbg.n)
                    pt_xbar(2 * NCHUNK - 2)
                    pt_xbar(2 * NCHUNK - 1)

                @blk.gpsimd
                def _(g):
                    for c in range(NCHUNK):
                        if c >= 1:
                            g.wait_ge(a_av.h, 8 * c)
                        a_vl.inc(g.dma_start(
                            out=vf[:],
                            in_=vtm[256 * c:256 * c + 512, :].rearrange("(kvc p) f -> p kvc f", p=128)), 16)
                        g.wait_ge(a_ev.h, 8 * (c + 1))
                        if c >= 2:
                            g.wait_ge(a_os.h, 16 * (c - 1))
                        if dbg4 and c == 2:
                            a_dbg.inc(g.dma_start(out=osb_d.rearrange("p a b -> p (a b)"),
                                                  in_=osb[c % 2][:].rearrange("p a b -> p (a b)")), 16)
                        a_os.inc(g.dma_start(out=o_f[:, :, 256 * c:256 * (c + 1)],
                                             in_=osb[c % 2][:]), 16)
                    g.wait_ge(a_os.h, 16 * NCHUNK)

                @blk.tensor
                def _(t):
                    def emit_av(hp):
                        c = hp // 2
                        t.wait_ge(a_vl.h, 16 * (c + 1))
                        t.wait_ge(a_px.h, 16 * (hp + 1))
                        for g4 in range(4):
                            ga = 4 * hp + g4
                            if ga >= 2:
                                t.wait_ge(a_ev.h, ga - 1)
                            pa = ps_a[g4 % 2]
                            for hh in range(4):
                                h = 4 * g4 + hh
                                for kvc in range(4):
                                    mm = t.matmul(pa[:, 128 * hh:128 * (hh + 1)],
                                                  vf[:, kvc, 128 * h:128 * (h + 1)],
                                                  pT[:, 4 * h + kvc, :],
                                                  start=(kvc == 0), stop=(kvc == 3))
                            a_av.inc(mm, 1)

                    for hp in range(2 * NCHUNK):
                        c, half = divmod(hp, 2)
                        if half == 0:
                            t.wait_ge(a_ld.h, 16 * 6 * (c + 1))
                        for gi in range(GH):
                            gg = GH * hp + gi
                            if gg >= 2:
                                t.wait_ge(a_ex.h, gg - 1)
                            for hh in range(2):
                                h = 2 * gi + hh
                                sl = ps_s[:, 1024 * (gi % 2) + 512 * hh:
                                          1024 * (gi % 2) + 512 * (hh + 1)]
                                t.matmul(sl[:, 0:256], qf[:, half, h, :], kf[:, 0:2, h, :],
                                         start=True, stop=False)
                                mm = t.matmul(sl[:, 256:512], qf[:, half, h, :], kf[:, 2:4, h, :],
                                              start=False, stop=True)
                            a_sc.inc(mm, 1)
                        if hp >= 1:
                            emit_av(hp - 1)
                    emit_av(2 * NCHUNK - 1)

                @blk.vector
                def _(v):
                    def evacs(hp):
                        c, half = divmod(hp, 2)
                        for g4 in range(4):
                            ga = 4 * hp + g4
                            v.wait_ge(a_av.h, ga + 1)
                            if c >= 2 and half == 0 and g4 == 0:
                                v.wait_ge(a_os.h, 16 * (c - 1))
                            a_ev.inc(v.tensor_copy(
                                osb[c % 2][:, 4 * g4:4 * (g4 + 1), 128 * half:128 * (half + 1)],
                                ps_a[g4 % 2][:].rearrange("p (h x) -> p h x", h=4)), 1)

                    for hp in range(2 * NCHUNK):
                        c, half = divmod(hp, 2)
                        msk = m0 if c == 0 else mB
                        for gi in range(GH):
                            gg = GH * hp + gi
                            v.wait_ge(a_sc.h, gg + 1)
                            sl = ps_s[:, 1024 * (gi % 2):1024 * (gi % 2) + 1024]
                            a_nm.inc(v.tensor_reduce(
                                nmx[:, 2 * gi:2 * gi + 2],
                                sl.rearrange("p (h x) -> p h x", h=2),
                                AX.X, ALU.max, negate=True), 1)
                            v.wait_ge(a_ex.h, gg + 1)
                            for hh in range(2):
                                h = 2 * gi + hh
                                last = v.tensor_tensor(
                                    pr2[:, h, :], pr[hp % 2][:, h, :],
                                    msk[:, half, :], ALU.mult)
                        a_fv.inc(last, 1)
                        v.wait_ge(a_fv.h, a_fv.n)
                        a_fv.inc(v.tensor_reduce(sm[:], pr2[:], AX.X, ALU.add), 1)
                        v.wait_ge(a_fv.h, a_fv.n)
                        a_fv.inc(v.reciprocal(rs[:], sm[:]), 1)
                        v.wait_ge(a_fv.h, a_fv.n)
                        for h in range(NH):
                            last = v.tensor_scalar(pr[hp % 2][:, h, :], pr2[:, h, :],
                                                   rs[:, h:h + 1], None, ALU.mult)
                        a_fv.inc(last, 1)
                        v.wait_ge(a_fv.h, a_fv.n)
                        a_nr.inc(v.tensor_copy(dmy[:], rs[:, 0:1]), 1)
                        if hp >= 1:
                            evacs(hp - 1)
                    evacs(2 * NCHUNK - 1)

                @blk.scalar
                def _(sc):
                    for hp in range(2 * NCHUNK):
                        if hp >= 2:
                            sc.wait_ge(a_px.h, 16 * (hp - 1))
                        for gi in range(GH):
                            gg = GH * hp + gi
                            sc.wait_ge(a_nm.h, gg + 1)
                            for hh in range(2):
                                h = 2 * gi + hh
                                sl = ps_s[:, 1024 * (gi % 2) + 512 * hh:
                                          1024 * (gi % 2) + 512 * (hh + 1)]
                                last = sc.activation(pr[hp % 2][:, h, :], sl, AF.Exp,
                                                     bias=nmx[:, h:h + 1], scale=1.0)
                            a_ex.inc(last, 1)

        cst.close()

        if phases >= 7:
            with ExitStack() as pst, nc.Block() as blk:
                pe_ = pst.enter_context
                wbufB = pe_(nc.sbuf_tensor("wbufB", [128, KC, D], BF16))
                wstgB = [pe_(nc.sbuf_tensor(f"wstgB{i}", [128, D], BF16)) for i in range(2)]
                ost = [pe_(nc.sbuf_tensor(f"b_o{i}", [128, KC, 128], BF16)) for i in range(2)]
                xr = [pe_(nc.sbuf_tensor(f"b_x{i}", [128, D], F32)) for i in range(2)]
                nz = pe_(nc.sbuf_tensor("b_nz", [128, D], F32))
                x2t = [pe_(nc.sbuf_tensor(f"b_2{i}", [128, D], F32)) for i in range(2)]
                bn6 = pe_(nc.sbuf_tensor("b_bn", [128, 24], F32))
                mv = pe_(nc.sbuf_tensor("b_mv", [128, 2], F32))
                vep = pe_(nc.sbuf_tensor("b_ve", [128, 1], F32))
                ngm = pe_(nc.sbuf_tensor("b_ng", [128, 1], F32))
                alp = pe_(nc.sbuf_tensor("b_al", [128, 1], F32))
                bet = pe_(nc.sbuf_tensor("b_be", [128, 1], F32))
                lnt = pe_(nc.sbuf_tensor("b_lt", [128, 1], F32))
                ps = [pe_(nc.psum_tensor(f"b_p{i}", [128, D], F32)) for i in range(2)]
                b_w1 = Sem(pst, nc, "b_w1")
                b_w2 = Sem(pst, nc, "b_w2")
                b_x = Sem(pst, nc, "b_x")
                b_pe = Sem(pst, nc, "b_pe")
                b_st = Sem(pst, nc, "b_st")
                b_nm = Sem(pst, nc, "b_nm")
                b_ad = Sem(pst, nc, "b_ad")
                b_so = Sem(pst, nc, "b_so")
                b_fv = Sem(pst, nc, "b_fv")
                b_fs = Sem(pst, nc, "b_fs")

                @blk.gpsimd
                def _(g):
                    for j in range(KC):
                        if j >= 2:
                            g.wait_ge(b_w2.h, 16 * (j - 1))
                        b_w1.inc(g.dma_start(out=wstgB[j % 2][:],
                                             in_=w_att[3][128 * j:128 * (j + 1), :]), 16)
                    for tt in range(TT_OWN):
                        b = tt % 2
                        if tt >= 2:
                            g.wait_ge(b_pe.h, 4 * (tt - 1))
                        b_x.inc(g.dma_start(out=ost[b][:], in_=o_f[:, :, 128 * tt:128 * (tt + 1)]), 16)
                        if tt >= 2:
                            g.wait_ge(b_ad.h, tt - 1)
                        b_x.inc(g.dma_start(out=xr[b][:], in_=x_til[:, tt + 2, :]), 16)

                @blk.sync
                def _(sync):
                    for j in range(KC):
                        sync.wait_ge(b_w1.h, 16 * (j + 1))
                        b_w2.inc(sync.dma_start_transpose(
                            wbufB[:, :, 128 * j:128 * (j + 1)], wstgB[j % 2][:]), 16)
                    for tt in range(TT_OWN):
                        b = tt % 2
                        sync.wait_ge(b_ad.h, tt + 1)
                        if tt >= 2:
                            sync.wait_ge(b_so.h, 16 * (tt - 1))
                        b_so.inc(sync.dma_start(out=x2m[128 * tt:128 * (tt + 1), :], in_=x2t[b][:]), 16)
                    sync.wait_ge(b_so.h, 16 * TT_OWN)

                @blk.tensor
                def _(t):
                    t.wait_ge(b_w2.h, 16 * KC)
                    for tt in range(TT_OWN):
                        b = tt % 2
                        t.wait_ge(b_x.h, 32 * tt + 16)
                        if tt >= 2:
                            t.wait_ge(b_nm.h, tt - 1)
                        for ow in range(4):
                            for kc in range(KC):
                                mm = t.matmul(ps[b][:, 512 * ow:512 * (ow + 1)],
                                              ost[b][:, kc, :],
                                              wbufB[:, kc, 512 * ow:512 * (ow + 1)],
                                              start=(kc == 0), stop=(kc == KC - 1))
                            b_pe.inc(mm, 1)

                @blk.vector
                def _(v):
                    for tt in range(TT_OWN):
                        b = tt % 2
                        v.wait_ge(b_pe.h, 4 * (tt + 1))
                        last = None
                        for q4 in range(4):
                            last = v.bn_stats(bn6[:, 6 * q4:6 * (q4 + 1)],
                                              ps[b][:, 512 * q4:512 * (q4 + 1)])
                        b_fv.inc(last, 1); v.wait_ge(b_fv.h, b_fv.n)
                        b_fv.inc(v.bn_aggr(mv[:], bn6[:]), 1); v.wait_ge(b_fv.h, b_fv.n)
                        v.tensor_scalar(ngm[:], mv[:, 0:1], -1.0, None, ALU.mult)
                        b_st.inc(v.tensor_scalar(vep[:], mv[:, 1:2], 1e-5, None, ALU.add), 1)
                        v.wait_ge(b_nm.h, tt + 1)
                        v.wait_ge(b_x.h, 32 * (tt + 1))
                        if tt >= 2:
                            v.wait_ge(b_so.h, 16 * (tt - 1))
                        b_ad.inc(v.tensor_tensor(x2t[b][:], nz[:], xr[b][:], ALU.add), 1)

                @blk.scalar
                def _(sc):
                    for tt in range(TT_OWN):
                        b = tt % 2
                        sc.wait_ge(b_st.h, tt + 1)
                        b_fs.inc(sc.activation(lnt[:], vep[:], AF.Ln, bias=0.0, scale=1.0), 1)
                        sc.wait_ge(b_fs.h, b_fs.n)
                        b_fs.inc(sc.activation(alp[:], lnt[:], AF.Exp, bias=0.0, scale=-0.5), 1)
                        sc.wait_ge(b_fs.h, b_fs.n)
                        b_fs.inc(sc.activation(bet[:], ngm[:], AF.Identity, bias=0.0, scale=alp[:]), 1)
                        sc.wait_ge(b_fs.h, b_fs.n)
                        if tt >= 1:
                            sc.wait_ge(b_ad.h, tt)
                        b_nm.inc(sc.activation(nz[:], ps[b][:], AF.Identity,
                                               bias=bet[:], scale=alp[:]), 1)

        if phases >= 8:
            with ExitStack() as pst, nc.Block() as blk:
                pe_ = pst.enter_context
                a2f = pe_(nc.sbuf_tensor("f_a2", [128, KC, 512], BF16))
                h_sb = pe_(nc.sbuf_tensor("f_h", [128, IC, 512], BF16))
                wg_g = [pe_(nc.sbuf_tensor(f"f_wg{i}", [128, KC, 128], BF16)) for i in range(3)]
                wg_u = [pe_(nc.sbuf_tensor(f"f_wu{i}", [128, KC, 128], BF16)) for i in range(3)]
                wdl = [pe_(nc.sbuf_tensor(f"f_wd{i}", [128, IC, 128], BF16)) for i in range(3)]
                sg = [pe_(nc.sbuf_tensor(f"f_sg{i}", [128, 512], BF16)) for i in range(2)]
                y_fs = [pe_(nc.sbuf_tensor(f"f_yf{i}", [128, 512], BF16)) for i in range(2)]
                y_tm = pe_(nc.sbuf_tensor("f_yt", [128, 4, D], BF16))
                x2i = [pe_(nc.sbuf_tensor(f"f_x2{i}", [128, D], F32)) for i in range(2)]
                x2r = [pe_(nc.sbuf_tensor(f"f_xr{i}", [128, D], F32)) for i in range(2)]
                a2t = [pe_(nc.sbuf_tensor(f"f_a2t{i}", [128, D], BF16)) for i in range(2)]
                ny = pe_(nc.sbuf_tensor("f_ny", [128, D], F32))
                x3 = [pe_(nc.sbuf_tensor(f"f_x3{i}", [128, D], F32)) for i in range(2)]
                bn6 = pe_(nc.sbuf_tensor("f_bn", [128, 24], F32))
                mv = pe_(nc.sbuf_tensor("f_mv", [128, 2], F32))
                vep = pe_(nc.sbuf_tensor("f_ve", [128, 1], F32))
                ngm = pe_(nc.sbuf_tensor("f_ng", [128, 1], F32))
                alp = pe_(nc.sbuf_tensor("f_al", [128, 1], F32))
                bet = pe_(nc.sbuf_tensor("f_be", [128, 1], F32))
                lnt = pe_(nc.sbuf_tensor("f_lt", [128, 1], F32))
                bn62 = pe_(nc.sbuf_tensor("f_bn2", [128, 24], F32))
                mv2 = pe_(nc.sbuf_tensor("f_mv2", [128, 2], F32))
                ngm2 = pe_(nc.sbuf_tensor("f_ng2", [128, 1], F32))
                alp2 = pe_(nc.sbuf_tensor("f_al2", [128, 1], F32))
                bet2 = pe_(nc.sbuf_tensor("f_be2", [128, 1], F32))
                lnt2 = pe_(nc.sbuf_tensor("f_lt2", [128, 1], F32))
                u1 = pe_(nc.sbuf_tensor("f_u1", [128, 1], F32))
                u2 = pe_(nc.sbuf_tensor("f_u2", [128, 1], F32))
                ps_g = [pe_(nc.psum_tensor(f"f_pg{i}", [128, 512], F32)) for i in range(2)]
                ps_u = [pe_(nc.psum_tensor(f"f_pu{i}", [128, 512], F32)) for i in range(2)]
                ps_y = [pe_(nc.psum_tensor(f"f_py{i}", [128, 512], F32)) for i in range(2)]

                f_x2 = Sem(pst, nc, "f_x2")
                f_st = Sem(pst, nc, "f_st")
                f_nm = Sem(pst, nc, "f_nm")
                f_ax = Sem(pst, nc, "f_ax")
                f_wg = Sem(pst, nc, "f_wg")
                f_wd = Sem(pst, nc, "f_wd")
                f_pe = Sem(pst, nc, "f_pe")
                f_si = Sem(pst, nc, "f_si")
                f_h = Sem(pst, nc, "f_h")
                f_yp = Sem(pst, nc, "f_yp")
                f_ye = Sem(pst, nc, "f_ye")
                f_yx = Sem(pst, nc, "f_yx")
                f_fs = Sem(pst, nc, "f_fs")
                f_fn = Sem(pst, nc, "f_fn")
                f_fa = Sem(pst, nc, "f_fa")
                f_xr = Sem(pst, nc, "f_xr")
                f_fv = Sem(pst, nc, "f_fv")
                f_fc = Sem(pst, nc, "f_fc")

                NT = nt

                @blk.gpsimd
                def _(g):
                    def wg_load(gp, p):
                        if gp >= 3:
                            g.wait_ge(f_pe.h, 2 * (gp - 2))
                        f_wg.inc(g.dma_start(out=wg_g[gp % 3][:], in_=wgu_t[p]), 16)
                        f_wg.inc(g.dma_start(out=wg_u[gp % 3][:], in_=wgu_t[p + IC]), 16)

                    for ti in range(NT):
                        lo = 3 if ti > 0 else 0
                        for p in range(lo, IC):
                            wg_load(IC * ti + p, p)
                        if ti + 1 < NT:
                            for p in range(3):
                                wg_load(IC * (ti + 1) + p, p)
                        for hc in range(HOC):
                            gh = HOC * ti + hc
                            if gh >= 3:
                                g.wait_ge(f_yp.h, gh - 2)
                            f_wd.inc(g.dma_start(out=wdl[gh % 3][:], in_=wd_t[hc]), 16)

                @blk.scalar
                def _(sc):
                    def ln2_group(tj):
                        for stt in range(4):
                            u = 4 * tj + stt
                            if u >= 2:
                                sc.wait_ge(f_nm.h, u - 1)
                            f_x2.inc(sc.dma_start(
                                out=x2i[u % 2][:],
                                in_=x2m[512 * tj + 128 * stt:512 * tj + 128 * (stt + 1), :]), 16)
                            sc.wait_ge(f_st.h, u + 1)
                            f_fc.inc(sc.activation(lnt[:], vep[:], AF.Ln, bias=0.0, scale=1.0), 1)
                            sc.wait_ge(f_fc.h, f_fc.n)
                            f_fc.inc(sc.activation(alp[:], lnt[:], AF.Exp, bias=0.0, scale=-0.5), 1)
                            sc.wait_ge(f_fc.h, f_fc.n)
                            f_fc.inc(sc.activation(bet[:], ngm[:], AF.Identity, bias=0.0, scale=alp[:]), 1)
                            sc.wait_ge(f_fc.h, f_fc.n)
                            if u >= 2:
                                sc.wait_ge(f_ax.h, 16 * (u - 1))
                            a2tv = a2t[u % 2][:].rearrange("t (j p) -> t j p", j=4)
                            x2iv = x2i[u % 2][:].rearrange("t (p j) -> t j p", j=4)
                            f_nm.inc(sc.activation(a2tv, x2iv, AF.Identity,
                                                   bias=bet[:], scale=alp[:]), 1)

                    ln2_group(0)
                    for ti in range(NT):
                        for p in range(IC):
                            gp = IC * ti + p
                            sc.wait_ge(f_pe.h, 2 * gp + 1)
                            if gp >= 2:
                                sc.wait_ge(f_h.h, gp - 1)
                            f_si.inc(sc.activation(sg[gp % 2][:], ps_g[gp % 2][:], AF.Silu), 1)
                        if ti + 1 < NT:
                            ln2_group(ti + 1)
                        for stt in range(4):
                            u = 4 * ti + stt
                            sc.wait_ge(f_fs.h, u + 1)
                            f_fc.inc(sc.activation(lnt2[:], u2[:], AF.Ln, bias=0.0, scale=1.0), 1)
                            sc.wait_ge(f_fc.h, f_fc.n)
                            f_fc.inc(sc.activation(alp2[:], lnt2[:], AF.Exp, bias=0.0, scale=-0.5), 1)
                            sc.wait_ge(f_fc.h, f_fc.n)
                            f_fc.inc(sc.activation(bet2[:], ngm2[:], AF.Identity, bias=0.0, scale=alp2[:]), 1)
                            sc.wait_ge(f_fc.h, f_fc.n)
                            if u >= 1:
                                sc.wait_ge(f_fa.h, u)
                            f_fn.inc(sc.activation(ny[:], y_tm[:, stt, :], AF.Identity,
                                                   bias=bet2[:], scale=alp2[:]), 1)

                @blk.vector
                def _(v):
                    def stats_group(tj):
                        for stt in range(4):
                            u = 4 * tj + stt
                            v.wait_ge(f_x2.h, 16 * (u + 1))
                            last = None
                            for q4 in range(4):
                                last = v.bn_stats(bn6[:, 6 * q4:6 * (q4 + 1)],
                                                  x2i[u % 2][:, 512 * q4:512 * (q4 + 1)])
                            f_fv.inc(last, 1); v.wait_ge(f_fv.h, f_fv.n)
                            f_fv.inc(v.bn_aggr(mv[:], bn6[:]), 1); v.wait_ge(f_fv.h, f_fv.n)
                            v.tensor_scalar(ngm[:], mv[:, 0:1], -1.0, None, ALU.mult)
                            f_st.inc(v.tensor_scalar(vep[:], mv[:, 1:2], 1e-5, None, ALU.add), 1)

                    stats_group(0)
                    for ti in range(NT):
                        for p in range(IC):
                            gp = IC * ti + p
                            if ti >= 1 and p == 0:
                                v.wait_ge(f_yp.h, HOC * ti)
                            v.wait_ge(f_si.h, gp + 1)
                            v.wait_ge(f_pe.h, 2 * gp + 2)
                            f_h.inc(v.tensor_tensor(h_sb[:, p, :], sg[gp % 2][:],
                                                    ps_u[gp % 2][:], ALU.mult), 1)
                        if ti + 1 < NT:
                            stats_group(ti + 1)
                        for hc in range(HOC):
                            gh = HOC * ti + hc
                            v.wait_ge(f_yp.h, gh + 1)
                            if gh >= 2:
                                v.wait_ge(f_yx.h, 16 * (gh - 1))
                            f_ye.inc(v.tensor_copy(y_fs[gh % 2][:], ps_y[gh % 2][:]), 1)
                        for stt in range(4):
                            u = 4 * ti + stt
                            v.wait_ge(f_yx.h, 16 * HOC * (ti + 1))
                            last = None
                            for q4 in range(4):
                                last = v.bn_stats(bn62[:, 6 * q4:6 * (q4 + 1)],
                                                  y_tm[:, stt, 512 * q4:512 * (q4 + 1)])
                            f_fv.inc(last, 1); v.wait_ge(f_fv.h, f_fv.n)
                            f_fv.inc(v.bn_aggr(mv2[:], bn62[:]), 1); v.wait_ge(f_fv.h, f_fv.n)
                            v.tensor_scalar(ngm2[:], mv2[:, 0:1], -1.0, None, ALU.mult)
                            f_fv.inc(v.tensor_scalar(u1[:], mv2[:, 0:1], mv2[:, 0:1], mv2[:, 1:2],
                                                     ALU.mult, ALU.add), 1)
                            v.wait_ge(f_fv.h, f_fv.n)
                            f_fv.inc(v.tensor_scalar(u2[:], u1[:], 1e-5, mv2[:, 1:2],
                                                     ALU.mult, ALU.add), 1)
                            v.wait_ge(f_fv.h, f_fv.n)
                            f_fs.inc(v.tensor_scalar(u2[:], u2[:], 1e-11, None, ALU.add), 1)
                            v.wait_ge(f_fn.h, u + 1)
                            v.wait_ge(f_xr.h, 16 * (u + 1))
                            if u >= 2:
                                v.wait_ge(s_fin.h, 16 * (u - 1))
                            f_fa.inc(v.tensor_tensor(x3[u % 2][:], ny[:], x2r[u % 2][:], ALU.add), 1)

                @blk.sync
                def _(sync):
                    def a2_xbars(tj):
                        for stt in range(4):
                            u = 4 * tj + stt
                            sync.wait_ge(f_nm.h, u + 1)
                            if tj >= 1 and stt == 0:
                                sync.wait_ge(f_pe.h, 2 * IC * tj)
                            f_ax.inc(sync.dma_start_transpose(
                                a2f[:, :, 128 * stt:128 * (stt + 1)], a2t[u % 2][:]), 16)

                    a2_xbars(0)
                    for ti in range(NT):
                        if ti + 1 < NT:
                            a2_xbars(ti + 1)
                        for hc in range(HOC):
                            gh = HOC * ti + hc
                            sync.wait_ge(f_ye.h, gh + 1)
                            if ti >= 1 and hc == 0:
                                sync.wait_ge(f_fn.h, 4 * ti)
                            f_yx.inc(sync.dma_start_transpose(
                                y_tm[:, :, 128 * hc:128 * (hc + 1)], y_fs[gh % 2][:]), 16)
                        for stt in range(4):
                            u = 4 * ti + stt
                            if u >= 2:
                                sync.wait_ge(f_fa.h, u - 1)
                            f_xr.inc(sync.dma_start(
                                out=x2r[u % 2][:],
                                in_=x2m[512 * ti + 128 * stt:512 * ti + 128 * (stt + 1), :]), 16)
                        for stt in range(4):
                            u = 4 * ti + stt
                            sync.wait_ge(f_fa.h, u + 1)
                            s_fin.inc(sync.dma_start(
                                out=out[512 * ti + 128 * stt:512 * ti + 128 * (stt + 1), :],
                                in_=x3[u % 2][:]), 16)
                    sync.wait_ge(s_fin.h, 16 * 4 * NT)

                @blk.tensor
                def _(t):
                    for ti in range(NT):
                        t.wait_ge(f_ax.h, 16 * 4 * (ti + 1))
                        for p in range(IC):
                            gp = IC * ti + p
                            t.wait_ge(f_wg.h, 32 * (gp + 1))
                            if gp >= 2:
                                t.wait_ge(f_si.h, gp - 1)
                                t.wait_ge(f_h.h, gp - 1)
                            for kc in range(KC):
                                mm = t.matmul(ps_g[gp % 2][:], wg_g[gp % 3][:, kc, :],
                                              a2f[:, kc, :], start=(kc == 0), stop=(kc == KC - 1))
                            f_pe.inc(mm, 1)
                            for kc in range(KC):
                                mm = t.matmul(ps_u[gp % 2][:], wg_u[gp % 3][:, kc, :],
                                              a2f[:, kc, :], start=(kc == 0), stop=(kc == KC - 1))
                            f_pe.inc(mm, 1)
                        for hc in range(HOC):
                            gh = HOC * ti + hc
                            t.wait_ge(f_wd.h, 16 * (gh + 1))
                            t.wait_ge(f_h.h, IC * (ti + 1))
                            if gh >= 2:
                                t.wait_ge(f_ye.h, gh - 1)
                            for ic in range(IC):
                                mm = t.matmul(ps_y[gh % 2][:], wdl[gh % 3][:, ic, :],
                                              h_sb[:, ic, :], start=(ic == 0), stop=(ic == IC - 1))
                            f_yp.inc(mm, 1)

        st.close()
        return nc
    return nc


def _host_prep(L0_attn_x, Wq, Wk, Wv, Wo, gup_a, gus_a, dnp_a, dns_a):
    x = np.asarray(L0_attn_x, dtype=np.float32).reshape(16384, 2048)
    inv = 1.0 / (10000.0 ** (np.arange(0, 128, 2, dtype=np.float64) / 128.0))
    dns_pad = np.zeros((48, 2048), dtype=np.float16)
    dns_pad[0:44] = np.asarray(dns_a, dtype=np.float16)
    scale = 1.0 / np.sqrt(128.0)
    Wq = np.ascontiguousarray(np.asarray(Wq, np.float32))
    Wk = np.ascontiguousarray(np.asarray(Wk, np.float32))
    Wv = np.ascontiguousarray(np.asarray(Wv, np.float32))
    Wo = np.ascontiguousarray(np.asarray(Wo, np.float32))
    gup_a = np.ascontiguousarray(np.asarray(gup_a, np.int32)
                                 .reshape(2, 1408, 4, 256).transpose(0, 2, 1, 3)
                                 .reshape(11264, 256))
    gus_a = np.ascontiguousarray(np.asarray(gus_a, np.float16)
                                 .reshape(16, 2, 1408, 4).transpose(0, 1, 3, 2)
                                 .reshape(16, 11264))
    dnp_a = np.ascontiguousarray(np.asarray(dnp_a, np.int32))
    tril = np.tril(np.ones((256, 256), np.float32))
    base = np.zeros((256, 512), np.float32)
    base[:, 0:256] = 1.0
    base[:, 256:512] = tril
    first = np.zeros((256, 512), np.float32)
    first[:, 256:512] = tril
    mB_ = np.ascontiguousarray(base.reshape(2, 128, 512).transpose(1, 0, 2))
    mF_ = np.ascontiguousarray(first.reshape(2, 128, 512).transpose(1, 0, 2))
    in_maps = []
    for c in range(NCORES):
        lo = c * T_OWN
        halo = np.zeros((256, 2048), np.float32) if c == 0 else x[lo - 256:lo]
        x_slice = np.ascontiguousarray(np.concatenate([halo, x[lo:lo + T_OWN]], axis=0))
        pos_all = np.arange(lo - 256, lo + T_OWN, dtype=np.float64)
        if c == 0:
            pos_all = np.maximum(pos_all, 0.0)
        ang = np.outer(pos_all, inv)
        ck_t = np.ascontiguousarray(np.cos(ang).astype(np.float32).reshape(TT_ALL, 128, 64).transpose(1, 0, 2))
        sk_t = np.ascontiguousarray(np.sin(ang).astype(np.float32).reshape(TT_ALL, 128, 64).transpose(1, 0, 2))
        ang_o = ang[256:]
        cq_t = np.ascontiguousarray((np.cos(ang_o) * scale).astype(np.float32).reshape(TT_OWN, 128, 64).transpose(1, 0, 2))
        sq_t = np.ascontiguousarray((np.sin(ang_o) * scale).astype(np.float32).reshape(TT_OWN, 128, 64).transpose(1, 0, 2))
        in_maps.append({
            "x": x_slice, "wq": Wq, "wk": Wk, "wv": Wv, "wo": Wo,
            "gup": gup_a, "gus": gus_a, "dnp": dnp_a, "dns": dns_pad,
            "cq": cq_t, "sq": sq_t, "ck": ck_t, "sk": sk_t,
            "m0": (mF_ if c == 0 else mB_), "mB": mB_,
        })
    return in_maps


def kernel(L0_attn_x, L0_attn_Wq, L0_attn_Wk, L0_attn_Wv, L0_attn_Wo,
           L0_ffn_gate_up_packed, L0_ffn_gate_up_scales,
           L0_ffn_down_packed, L0_ffn_down_scales):
    in_maps = _host_prep(L0_attn_x, L0_attn_Wq, L0_attn_Wk, L0_attn_Wv, L0_attn_Wo,
                         L0_ffn_gate_up_packed, L0_ffn_gate_up_scales,
                         L0_ffn_down_packed, L0_ffn_down_scales)
    nc = build()
    res = run_bass_kernel_spmd(nc, in_maps, core_ids=list(range(NCORES)))
    out = np.concatenate([r["out"] for r in res.results], axis=0)
    return out.reshape(1, 16384, 2048)


# version 15 (final)
# speedup vs baseline: 1.0796x; 1.0513x over previous
import numpy as np
from contextlib import ExitStack

import concourse.bass as bass
import concourse.mybir as mybir
from concourse.bass_utils import run_bass_kernel_spmd

F32 = mybir.dt.float32
BF16 = mybir.dt.bfloat16
FP16 = mybir.dt.float16
U32 = mybir.dt.uint32
U16 = mybir.dt.uint16
I32 = mybir.dt.int32
AF = mybir.ActivationFunctionType
ALU = mybir.AluOpType
AX = mybir.AxisListType

NCORES = 8
D = 2048
KC = 16
NH = 16
W = 256
T_OWN = 2048
T_ALL = 2304
TT_OWN = 16
TT_ALL = 18
NCHUNK = 8
I_FFN = 5632
IC = 44
GU = 11264
GOC = 88
HOC = 16

SIGN_MASK = 0x80008000 - (1 << 32)
EEM_MASK = 0x0E000E00


class Sem:
    def __init__(self, st, nc, name):
        self.h = st.enter_context(nc.semaphore(name))
        self.n = 0

    def inc(self, call, k=1):
        call.then_inc(self.h, k)
        self.n += k
        return self.n


def build(phases=9, debug=(), nt=4):
    nc = bass.Bass("TRN2", target_bir_lowering=False)

    x_in = nc.dram_tensor("x", [T_ALL, D], F32, kind="ExternalInput").ap()
    w_att = [nc.dram_tensor(n, [D, D], F32, kind="ExternalInput").ap()
             for n in ("wq", "wk", "wv", "wo")]
    gup = nc.dram_tensor("gup", [GU, 256], I32, kind="ExternalInput").ap()
    gus = nc.dram_tensor("gus", [16, GU], FP16, kind="ExternalInput").ap()
    dnp = nc.dram_tensor("dnp", [D, 704], I32, kind="ExternalInput").ap()
    dns = nc.dram_tensor("dns", [48, D], FP16, kind="ExternalInput").ap()
    cq_in = nc.dram_tensor("cq", [128, TT_OWN, 64], F32, kind="ExternalInput").ap()
    sq_in = nc.dram_tensor("sq", [128, TT_OWN, 64], F32, kind="ExternalInput").ap()
    ck_in = nc.dram_tensor("ck", [128, TT_ALL, 64], F32, kind="ExternalInput").ap()
    sk_in = nc.dram_tensor("sk", [128, TT_ALL, 64], F32, kind="ExternalInput").ap()
    m0_in = nc.dram_tensor("m0", [128, 2, 512], F32, kind="ExternalInput").ap()
    mB_in = nc.dram_tensor("mB", [128, 2, 512], F32, kind="ExternalInput").ap()

    out = nc.dram_tensor("out", [T_OWN, D], F32, kind="ExternalOutput").ap()

    def scratch(name, shape, dt):
        kind = "ExternalOutput" if (name in debug) else "Internal"
        return nc.dram_tensor(name, shape, dt, kind=kind).ap()

    a1f = scratch("a1f", [128, TT_ALL, KC, 128], BF16)
    qtm = scratch("qtm", [T_OWN, D], BF16)
    ktm = scratch("ktm", [T_ALL, D], BF16)
    vtm = scratch("vtm", [T_ALL, D], BF16)
    o_f = scratch("o_f", [128, KC, T_OWN], BF16)
    x2m = scratch("x2m", [T_OWN, D], F32)
    wgu_t = scratch("wgu_t", [GOC, 128, KC, 128], BF16)
    wd_t = scratch("wd_t", [HOC, 128, IC, 128], BF16)

    x_til = x_in.rearrange("(tt p) f -> p tt f", p=128)

    st = ExitStack()
    with st:
        e = st.enter_context
        cst = ExitStack()
        ce = cst.enter_context
        cq = ce(nc.sbuf_tensor("cq_sb", [128, TT_OWN, 64], F32))
        sq = ce(nc.sbuf_tensor("sq_sb", [128, TT_OWN, 64], F32))
        ck = ce(nc.sbuf_tensor("ck_sb", [128, TT_ALL, 64], F32))
        sk = ce(nc.sbuf_tensor("sk_sb", [128, TT_ALL, 64], F32))
        m0 = ce(nc.sbuf_tensor("m0_sb", [128, 2, 512], BF16))
        mB = ce(nc.sbuf_tensor("mB_sb", [128, 2, 512], BF16))
        s_gu = ce(nc.sbuf_tensor("sgu_sb", [128, GOC, 16], FP16))
        s_dn = ce(nc.sbuf_tensor("sdn_sb", [128, HOC, 48], FP16))

        s_fin = Sem(st, nc, "s_fin")

        with ExitStack() as pst, nc.Block() as blk:
            pe_ = pst.enter_context
            m0f = pe_(nc.sbuf_tensor("m0f", [128, 2, 512], F32))
            mBf = pe_(nc.sbuf_tensor("mBf", [128, 2, 512], F32))
            gus_sb = pe_(nc.sbuf_tensor("gus_sb", [16, GU], FP16))
            dns_sb = pe_(nc.sbuf_tensor("dns_sb", [48, D], FP16))
            sc_in = Sem(pst, nc, "c_in")
            sc_xb = Sem(pst, nc, "c_xb")

            @blk.sync
            def _(sync):
                for dst, src in ((cq, cq_in), (sq, sq_in), (ck, ck_in), (sk, sk_in),
                                 (m0f, m0_in), (mBf, mB_in), (gus_sb, gus), (dns_sb, dns)):
                    sc_in.inc(sync.dma_start(out=dst[:], in_=src), 16)
                sync.wait_ge(sc_in.h, sc_in.n)
                sc_xb.inc(sync.dma_start_transpose(s_gu[:], gus_sb[:]), 16)
                sc_xb.inc(sync.dma_start_transpose(s_dn[:], dns_sb[:]), 16)
                sync.wait_ge(sc_xb.h, sc_xb.n)

            @blk.vector
            def _(v):
                v.wait_ge(sc_in.h, 128)
                v.tensor_copy(m0[:], m0f[:])
                v.tensor_copy(mB[:], mBf[:])

        mid = ExitStack()
        with mid:
            wbuf = [mid.enter_context(nc.sbuf_tensor(f"wbuf{i}", [128, KC, D], BF16))
                    for i in range(2)]
            wstg = [mid.enter_context(nc.sbuf_tensor(f"wstg{i}", [128, D], BF16))
                    for i in range(2)]
            s_wst = Sem(st, nc, "s_wst")
            s_wxb = Sem(st, nc, "s_wxb")

            def emit_wprep_g(g, widx, j):
                u = widx * KC + j
                if u >= 2:
                    g.wait_ge(s_wxb.h, 16 * (u - 1))
                s_wst.inc(g.dma_start(out=wstg[j % 2][:],
                                      in_=w_att[widx][128 * j:128 * (j + 1), :]), 16)

            def emit_wprep_x(sync, widx, buf, j):
                u = widx * KC + j
                sync.wait_ge(s_wst.h, 16 * (u + 1))
                s_wxb.inc(sync.dma_start_transpose(
                    wbuf[buf][:, :, 128 * j:128 * (j + 1)], wstg[j % 2][:]), 16)

            if phases >= 1:
                with ExitStack() as pst, nc.Block() as blk:
                    pe_ = pst.enter_context
                    xin = [pe_(nc.sbuf_tensor(f"p0x{i}", [128, D], F32)) for i in range(2)]
                    a1t = [pe_(nc.sbuf_tensor(f"p0a{i}", [128, D], BF16)) for i in range(2)]
                    xbo = [pe_(nc.sbuf_tensor(f"p0b{i}", [128, KC, 128], BF16)) for i in range(2)]
                    bn6 = [pe_(nc.sbuf_tensor(f"p0n{i}", [128, 24], F32)) for i in range(2)]
                    mv = [pe_(nc.sbuf_tensor(f"p0m{i}", [128, 2], F32)) for i in range(2)]
                    vep = [pe_(nc.sbuf_tensor(f"p0v{i}", [128, 1], F32)) for i in range(2)]
                    ngm = [pe_(nc.sbuf_tensor(f"p0g{i}", [128, 1], F32)) for i in range(2)]
                    alp = [pe_(nc.sbuf_tensor(f"p0A{i}", [128, 1], F32)) for i in range(2)]
                    bet = [pe_(nc.sbuf_tensor(f"p0B{i}", [128, 1], F32)) for i in range(2)]
                    lnt = [pe_(nc.sbuf_tensor(f"p0L{i}", [128, 1], F32)) for i in range(2)]
                    p_x = Sem(pst, nc, "p0_x")
                    p_fv = Sem(pst, nc, "p0_fv")
                    p_fs = Sem(pst, nc, "p0_fs")
                    p_st = Sem(pst, nc, "p0_st")
                    p_nm = Sem(pst, nc, "p0_nm")
                    p_xb = Sem(pst, nc, "p0_xb")
                    p_so = Sem(pst, nc, "p0_so")

                    @blk.gpsimd
                    def _(g):
                        for tt in range(TT_ALL):
                            b = tt % 2
                            if tt >= 2:
                                g.wait_ge(p_nm.h, tt - 1)
                            p_x.inc(g.dma_start(out=xin[b][:], in_=x_til[:, tt, :]), 16)
                            if tt < KC:
                                emit_wprep_g(g, 0, tt)

                    @blk.vector
                    def _(v):
                        for tt in range(TT_ALL):
                            b = tt % 2
                            v.wait_ge(p_x.h, 16 * (tt + 1))
                            last = None
                            for q4 in range(4):
                                last = v.bn_stats(bn6[b][:, 6 * q4:6 * (q4 + 1)],
                                                  xin[b][:, 512 * q4:512 * (q4 + 1)])
                            p_fv.inc(last, 1); v.wait_ge(p_fv.h, p_fv.n)
                            p_fv.inc(v.bn_aggr(mv[b][:], bn6[b][:]), 1); v.wait_ge(p_fv.h, p_fv.n)
                            v.tensor_scalar(ngm[b][:], mv[b][:, 0:1], -1.0, None, ALU.mult)
                            p_st.inc(v.tensor_scalar(vep[b][:], mv[b][:, 1:2], 1e-5, None, ALU.add), 1)

                    @blk.scalar
                    def _(sc):
                        for tt in range(TT_ALL):
                            b = tt % 2
                            sc.wait_ge(p_st.h, tt + 1)
                            p_fs.inc(sc.activation(lnt[b][:], vep[b][:], AF.Ln, bias=0.0, scale=1.0), 1)
                            sc.wait_ge(p_fs.h, p_fs.n)
                            p_fs.inc(sc.activation(alp[b][:], lnt[b][:], AF.Exp, bias=0.0, scale=-0.5), 1)
                            sc.wait_ge(p_fs.h, p_fs.n)
                            p_fs.inc(sc.activation(bet[b][:], ngm[b][:], AF.Identity, bias=0.0, scale=alp[b][:]), 1)
                            sc.wait_ge(p_fs.h, p_fs.n)
                            if tt >= 2:
                                sc.wait_ge(p_xb.h, 16 * (tt - 1))
                            p_nm.inc(sc.activation(a1t[b][:], xin[b][:], AF.Identity,
                                                   bias=bet[b][:], scale=alp[b][:]), 1)

                    @blk.sync
                    def _(sync):
                        for tt in range(TT_ALL):
                            b = tt % 2
                            sync.wait_ge(p_nm.h, tt + 1)
                            if tt >= 2:
                                sync.wait_ge(p_so.h, 16 * (tt - 1))
                            p_xb.inc(sync.dma_start_transpose(xbo[b][:], a1t[b][:]), 16)
                            sync.wait_ge(p_xb.h, 16 * (tt + 1))
                            p_so.inc(sync.dma_start(out=a1f[:, tt], in_=xbo[b][:]), 16)
                            if tt < KC:
                                emit_wprep_x(sync, 0, 0, tt)
                        sync.wait_ge(p_so.h, 16 * TT_ALL)

            def proj_phase(pid, wb, prep_widx, prep_buf, tts, rope, out_ap, tt0, wxb_need=0):
                with ExitStack() as pst, nc.Block() as blk:
                    pe_ = pst.enter_context
                    a1s = [pe_(nc.sbuf_tensor(f"q{pid}a{i}", [128, KC, 128], BF16)) for i in range(2)]
                    osb = [pe_(nc.sbuf_tensor(f"q{pid}o{i}", [128, D], BF16)) for i in range(2)]
                    tmp = [pe_(nc.sbuf_tensor(f"q{pid}t{i}", [128, 4, 64], F32)) for i in range(4)]
                    ps = [pe_(nc.psum_tensor(f"q{pid}p{i}", [128, D], F32)) for i in range(2)]
                    p_x = Sem(pst, nc, f"q{pid}_x")
                    p_pe = Sem(pst, nc, f"q{pid}_pe")
                    p_ev = Sem(pst, nc, f"q{pid}_ev")
                    p_so = Sem(pst, nc, f"q{pid}_so")
                    ot = out_ap.rearrange("(tt p) f -> p tt f", p=128)
                    n = len(tts)

                    @blk.gpsimd
                    def _(g):
                        for ti in range(n):
                            b = ti % 2
                            if ti >= 2:
                                g.wait_ge(p_pe.h, 4 * (ti - 1))
                            p_x.inc(g.dma_start(out=a1s[b][:], in_=a1f[:, tts[ti]]), 16)
                            if prep_widx is not None and ti < KC:
                                emit_wprep_g(g, prep_widx, ti)

                    @blk.tensor
                    def _(t):
                        t.wait_ge(s_wxb.h, wxb_need)
                        for ti in range(n):
                            b = ti % 2
                            t.wait_ge(p_x.h, 16 * (ti + 1))
                            if ti >= 2:
                                t.wait_ge(p_ev.h, 4 * (ti - 1))
                            for ow in range(4):
                                for kc in range(KC):
                                    mm = t.matmul(ps[b][:, 512 * ow:512 * (ow + 1)],
                                                  a1s[b][:, kc, :],
                                                  wb[:, kc, 512 * ow:512 * (ow + 1)],
                                                  start=(kc == 0), stop=(kc == KC - 1))
                                p_pe.inc(mm, 1)

                    @blk.vector
                    def _(v):
                        for ti in range(n):
                            b = ti % 2
                            for ow in range(4):
                                v.wait_ge(p_pe.h, 4 * ti + ow + 1)
                                if ti >= 2:
                                    v.wait_ge(p_so.h, 16 * (ti - 1))
                                win = ps[b][:, 512 * ow:512 * (ow + 1)].rearrange(
                                    "p (h x) -> p h x", h=4)
                                owin = osb[b][:, 512 * ow:512 * (ow + 1)].rearrange(
                                    "p (h x) -> p h x", h=4)
                                if rope is not None:
                                    c_sb, s_sb, toff = rope
                                    tti = tts[ti] - toff
                                    c_b = c_sb[:, tti:tti + 1, :].broadcast_to([128, 4, 64])
                                    s_b = s_sb[:, tti:tti + 1, :].broadcast_to([128, 4, 64])
                                    ev, od = win[:, :, 0::2], win[:, :, 1::2]
                                    v.tensor_tensor(tmp[0][:], ev, c_b, ALU.mult)
                                    v.tensor_tensor(tmp[1][:], od, s_b, ALU.mult)
                                    v.tensor_tensor(owin[:, :, 0::2], tmp[0][:], tmp[1][:], ALU.subtract)
                                    v.tensor_tensor(tmp[2][:], ev, s_b, ALU.mult)
                                    v.tensor_tensor(tmp[3][:], od, c_b, ALU.mult)
                                    last = v.tensor_tensor(owin[:, :, 1::2], tmp[2][:], tmp[3][:], ALU.add)
                                else:
                                    last = v.tensor_copy(osb[b][:, 512 * ow:512 * (ow + 1)],
                                                         ps[b][:, 512 * ow:512 * (ow + 1)])
                                p_ev.inc(last, 1)

                    @blk.sync
                    def _(sync):
                        for ti in range(n):
                            b = ti % 2
                            sync.wait_ge(p_ev.h, 4 * (ti + 1))
                            p_so.inc(sync.dma_start(out=ot[:, tts[ti] - tt0, :], in_=osb[b][:]), 16)
                            if prep_widx is not None and ti < KC:
                                emit_wprep_x(sync, prep_widx, prep_buf, ti)
                        sync.wait_ge(p_so.h, 16 * n)

            if phases >= 2:
                proj_phase(0, wbuf[0], 1, 1, list(range(2, TT_ALL)), (cq, sq, 2), qtm, 2, wxb_need=16 * KC)
            if phases >= 3:
                proj_phase(1, wbuf[1], 2, 0, list(range(TT_ALL)), (ck, sk, 0), ktm, 0, wxb_need=32 * KC)
            if phases >= 4:
                proj_phase(2, wbuf[0], None, None, list(range(TT_ALL)), None, vtm, 0, wxb_need=48 * KC)


        if phases >= 5:
            with ExitStack() as pst, nc.Block() as blk:
                pe_ = pst.enter_context
                dq_w = [pe_(nc.sbuf_tensor(f"dqw{i}", [128, 704], U32)) for i in range(2)]
                dq_A = pe_(nc.sbuf_tensor("dqA", [128, 1408], U16))
                dq_B = pe_(nc.sbuf_tensor("dqB", [128, 1408], U16))
                dq_C = pe_(nc.sbuf_tensor("dqC", [128, 1408], U16))
                s_ex = [pe_(nc.sbuf_tensor(f"dqs{i}", [128, 1408], FP16)) for i in range(2)]
                dqq = [pe_(nc.sbuf_tensor(f"dqq{i}", [128, I_FFN], BF16)) for i in range(2)]
                dq_o = [pe_(nc.sbuf_tensor(f"dqo{i}", [128, IC, 128], BF16)) for i in range(2)]
                d_in = Sem(pst, nc, "d_in")
                d_v = Sem(pst, nc, "d_v")
                d_xb = Sem(pst, nc, "d_xb")
                d_so = Sem(pst, nc, "d_so")
                NTOT = GOC + HOC

                def spec(i):
                    return ("gu", i) if i < GOC else ("dn", i - GOC)

                def emit_store(g, i):
                    kind, idx = spec(i)
                    b = i % 2
                    g.wait_ge(d_xb.h, 16 * (i + 1))
                    if kind == "gu":
                        d_so.inc(g.dma_start(out=wgu_t[idx], in_=dq_o[b][:, 0:KC, :]), 16)
                    else:
                        d_so.inc(g.dma_start(out=wd_t[idx], in_=dq_o[b][:, 0:IC, :]), 16)

                @blk.gpsimd
                def _(g):
                    for i in range(NTOT):
                        kind, idx = spec(i)
                        b = i % 2
                        if i >= 2:
                            g.wait_ge(d_v.h, i - 1)
                        if kind == "gu":
                            d_in.inc(g.dma_start(
                                out=dq_w[b][:, 0:256],
                                in_=gup[128 * idx:128 * (idx + 1), :].bitcast(U32)), 16)
                        else:
                            d_in.inc(g.dma_start(out=dq_w[b][:, 0:704],
                                                 in_=dnp[128 * idx:128 * (idx + 1), :].bitcast(U32)), 16)
                        if i >= 2:
                            emit_store(g, i - 2)
                    emit_store(g, NTOT - 2)
                    emit_store(g, NTOT - 1)
                    g.wait_ge(d_so.h, 16 * NTOT)

                @blk.vector
                def _(v):
                    for i in range(NTOT):
                        kind, idx = spec(i)
                        b = i % 2
                        nw = 256 if kind == "gu" else 704
                        nq = 2 * nw
                        ng = nq // 32
                        v.wait_ge(d_in.h, 16 * (i + 1))
                        if i >= 2:
                            v.wait_ge(d_xb.h, 16 * (i - 1))
                        if kind == "gu":
                            s_src = s_gu[:, idx, :].unsqueeze(2).broadcast_to([128, ng, 32])
                        else:
                            s_src = s_dn[:, idx, 0:44].unsqueeze(2).broadcast_to([128, ng, 32])
                        v.tensor_scalar(s_ex[b][:, 0:nq].rearrange("p (g x) -> p g x", g=ng),
                                        s_src, 16384.0, None, ALU.mult)
                        wv = dq_w[b][:, 0:nw].bitcast(U16)
                        for j in range(4):
                            if j < 3:
                                v.tensor_scalar(dq_A[:, 0:nq], wv, 9 - 4 * j, 0x0E00,
                                                ALU.logical_shift_left, ALU.bitwise_and)
                                v.tensor_scalar(dq_B[:, 0:nq], wv, 12 - 4 * j, 0x8000,
                                                ALU.logical_shift_left, ALU.bitwise_and)
                            else:
                                v.tensor_scalar(dq_A[:, 0:nq], wv, 3, 0x0E00,
                                                ALU.logical_shift_right, ALU.bitwise_and)
                                v.tensor_scalar(dq_B[:, 0:nq], wv, 0x8000, None,
                                                ALU.bitwise_and)
                            v.tensor_tensor(dq_C[:, 0:nq], dq_A[:, 0:nq], dq_B[:, 0:nq],
                                            ALU.bitwise_or)
                            last = v.tensor_tensor(dqq[b][:, j * nq:(j + 1) * nq],
                                                   dq_C[:, 0:nq].bitcast(FP16),
                                                   s_ex[b][:, 0:nq], ALU.mult)
                        d_v.inc(last, 1)

                @blk.sync
                def _(sync):
                    for i in range(NTOT):
                        kind, idx = spec(i)
                        b = i % 2
                        nk = 2048 if kind == "gu" else I_FFN
                        ncc = KC if kind == "gu" else IC
                        sync.wait_ge(d_v.h, i + 1)
                        if i >= 2:
                            sync.wait_ge(d_so.h, 16 * (i - 1))
                        d_xb.inc(sync.dma_start_transpose(dq_o[b][:, 0:ncc, :], dqq[b][:, 0:nk]), 16)

        if phases >= 6:
            dbg4 = "dbg4" in debug
            if dbg4:
                pr_d = nc.dram_tensor("pr_d", [128, NH, 512], BF16, kind="ExternalOutput").ap()
                pT_d = nc.dram_tensor("pT_d", [128, 64, 128], BF16, kind="ExternalOutput").ap()
                nmx_d = nc.dram_tensor("nmx_d", [128, NH], F32, kind="ExternalOutput").ap()
                sm_d = nc.dram_tensor("sm_d", [128, NH], F32, kind="ExternalOutput").ap()
                qf_d = nc.dram_tensor("qf_d", [128, 2, KC, 128], BF16, kind="ExternalOutput").ap()
                osb_d = nc.dram_tensor("osb_d", [128, NH, 256], BF16, kind="ExternalOutput").ap()
                kf_d = nc.dram_tensor("kf_d", [128, 4, KC, 128], BF16, kind="ExternalOutput").ap()
                vf_d = nc.dram_tensor("vf_d", [128, 4, D], BF16, kind="ExternalOutput").ap()
            with ExitStack() as pst, nc.Block() as blk:
                pe_ = pst.enter_context
                qf = [pe_(nc.sbuf_tensor(f"a_q{i}", [128, 2, KC, 128], BF16)) for i in range(2)]
                kf = [pe_(nc.sbuf_tensor(f"a_k{i}", [128, 4, KC, 128], BF16)) for i in range(2)]
                vf = pe_(nc.sbuf_tensor("a_v", [128, 4, D], BF16))
                pr = [pe_(nc.sbuf_tensor(f"a_p{i}", [128, NH, 512], BF16)) for i in range(2)]
                pr2 = pe_(nc.sbuf_tensor("a_p2", [128, NH, 512], BF16))
                pT = pe_(nc.sbuf_tensor("a_T", [128, 64, 128], BF16))
                osb = [pe_(nc.sbuf_tensor(f"a_o{i}", [128, NH, 256], BF16)) for i in range(2)]
                nmx = pe_(nc.sbuf_tensor("a_nm", [128, NH], F32))
                sm = pe_(nc.sbuf_tensor("a_sm", [128, NH], F32))
                rs = pe_(nc.sbuf_tensor("a_rs", [128, NH], F32))
                dmy = pe_(nc.sbuf_tensor("a_dmy", [128, 1], F32))
                smsnap = pe_(nc.sbuf_tensor("a_smsnap", [128, NH], F32))
                ps_s = pe_(nc.psum_tensor("a_ps", [128, 2048], F32))
                ps_a = [pe_(nc.psum_tensor(f"a_pa{i}", [128, 512], F32)) for i in range(2)]
                a_ld = Sem(pst, nc, "a_ld")
                a_vl = Sem(pst, nc, "a_vl")
                a_sc = Sem(pst, nc, "a_sc")
                a_nm = Sem(pst, nc, "a_nm")
                a_ex = Sem(pst, nc, "a_ex")
                a_st = Sem(pst, nc, "a_st")
                a_nr = Sem(pst, nc, "a_nr")
                a_px = Sem(pst, nc, "a_px")
                a_av = Sem(pst, nc, "a_av")
                a_ev = Sem(pst, nc, "a_ev")
                a_os = Sem(pst, nc, "a_os")
                a_fv = Sem(pst, nc, "a_fv")
                a_dbg = Sem(pst, nc, "a_dbg")
                GH = 8

                @blk.sync
                def _(sync):
                    def pt_xbar(hp):
                        sync.wait_ge(a_nr.h, hp + 1)
                        if hp >= 1:
                            sync.wait_ge(a_av.h, 4 * hp)
                        a_px.inc(sync.dma_start_transpose(
                            pT[:], pr[hp % 2][:].rearrange("p h x -> p (h x)")), 16)

                    for c in range(NCHUNK):
                        if c >= 2:
                            sync.wait_ge(a_sc.h, 2 * GH * (c - 1))
                        for hf in range(2):
                            a_ld.inc(sync.dma_start_transpose(
                                qf[c % 2][:, hf], qtm[256 * c + 128 * hf:256 * c + 128 * (hf + 1), :]), 16)
                        for hh in range(4):
                            a_ld.inc(sync.dma_start_transpose(
                                kf[c % 2][:, hh], ktm[256 * c + 128 * hh:256 * c + 128 * (hh + 1), :]), 16)
                        for hp in (2 * c - 2, 2 * c - 1):
                            if hp >= 0:
                                pt_xbar(hp)
                        for half in range(2):
                            hp = 2 * c + half
                            if dbg4 and False and c == 2 and half == 0:
                                sync.wait_ge(a_px.h, 16)
                                a_dbg.inc(sync.dma_start(out=pr_d.rearrange("p a b -> p (a b)"),
                                               in_=pr[0][:].rearrange("p a b -> p (a b)")), 16)
                                a_dbg.inc(sync.dma_start(out=pT_d.rearrange("p a b -> p (a b)"),
                                               in_=pT[:].rearrange("p a b -> p (a b)")), 16)
                                a_dbg.inc(sync.dma_start(out=nmx_d, in_=nmx[:]), 16)
                                a_dbg.inc(sync.dma_start(out=sm_d, in_=smsnap[:]), 16)
                                a_dbg.inc(sync.dma_start(out=qf_d.rearrange("p a b c -> p (a b c)"),
                                               in_=qf[:].rearrange("p a b c -> p (a b c)")), 16)
                                a_dbg.inc(sync.dma_start(out=kf_d.rearrange("p a b c -> p (a b c)"),
                                               in_=kf[:].rearrange("p a b c -> p (a b c)")), 16)
                                a_dbg.inc(sync.dma_start(out=vf_d.rearrange("p a b -> p (a b)"),
                                               in_=vf[:].rearrange("p a b -> p (a b)")), 16)
                                sync.wait_ge(a_dbg.h, a_dbg.n)
                    pt_xbar(2 * NCHUNK - 2)
                    pt_xbar(2 * NCHUNK - 1)

                @blk.gpsimd
                def _(g):
                    for c in range(NCHUNK):
                        if c >= 1:
                            g.wait_ge(a_av.h, 8 * c)
                        a_vl.inc(g.dma_start(
                            out=vf[:],
                            in_=vtm[256 * c:256 * c + 512, :].rearrange("(kvc p) f -> p kvc f", p=128)), 16)
                        g.wait_ge(a_ev.h, 8 * (c + 1))
                        if c >= 2:
                            g.wait_ge(a_os.h, 16 * (c - 1))
                        if dbg4 and c == 2:
                            a_dbg.inc(g.dma_start(out=osb_d.rearrange("p a b -> p (a b)"),
                                                  in_=osb[c % 2][:].rearrange("p a b -> p (a b)")), 16)
                        a_os.inc(g.dma_start(out=o_f[:, :, 256 * c:256 * (c + 1)],
                                             in_=osb[c % 2][:]), 16)
                    g.wait_ge(a_os.h, 16 * NCHUNK)

                @blk.tensor
                def _(t):
                    def emit_av(hp):
                        c = hp // 2
                        t.wait_ge(a_vl.h, 16 * (c + 1))
                        t.wait_ge(a_px.h, 16 * (hp + 1))
                        for g4 in range(4):
                            ga = 4 * hp + g4
                            if ga >= 2:
                                t.wait_ge(a_ev.h, ga - 1)
                            pa = ps_a[g4 % 2]
                            for hh in range(4):
                                h = 4 * g4 + hh
                                for kvc in range(4):
                                    mm = t.matmul(pa[:, 128 * hh:128 * (hh + 1)],
                                                  vf[:, kvc, 128 * h:128 * (h + 1)],
                                                  pT[:, 4 * h + kvc, :],
                                                  start=(kvc == 0), stop=(kvc == 3))
                            a_av.inc(mm, 1)

                    for hp in range(2 * NCHUNK):
                        c, half = divmod(hp, 2)
                        if half == 0:
                            t.wait_ge(a_ld.h, 16 * 6 * (c + 1))
                        for gi in range(GH):
                            gg = GH * hp + gi
                            if gg >= 2:
                                t.wait_ge(a_ex.h, gg - 1)
                            for hh in range(2):
                                h = 2 * gi + hh
                                sl = ps_s[:, 1024 * (gi % 2) + 512 * hh:
                                          1024 * (gi % 2) + 512 * (hh + 1)]
                                t.matmul(sl[:, 0:256], qf[c % 2][:, half, h, :], kf[c % 2][:, 0:2, h, :],
                                         start=True, stop=False)
                                mm = t.matmul(sl[:, 256:512], qf[c % 2][:, half, h, :], kf[c % 2][:, 2:4, h, :],
                                              start=False, stop=True)
                            a_sc.inc(mm, 1)
                        if hp >= 1:
                            emit_av(hp - 1)
                    emit_av(2 * NCHUNK - 1)

                @blk.vector
                def _(v):
                    def evacs(hp):
                        c, half = divmod(hp, 2)
                        for g4 in range(4):
                            ga = 4 * hp + g4
                            v.wait_ge(a_av.h, ga + 1)
                            if c >= 2 and half == 0 and g4 == 0:
                                v.wait_ge(a_os.h, 16 * (c - 1))
                            a_ev.inc(v.tensor_copy(
                                osb[c % 2][:, 4 * g4:4 * (g4 + 1), 128 * half:128 * (half + 1)],
                                ps_a[g4 % 2][:].rearrange("p (h x) -> p h x", h=4)), 1)

                    for hp in range(2 * NCHUNK):
                        c, half = divmod(hp, 2)
                        msk = m0 if c == 0 else mB
                        for gi in range(GH):
                            gg = GH * hp + gi
                            v.wait_ge(a_sc.h, gg + 1)
                            sl = ps_s[:, 1024 * (gi % 2):1024 * (gi % 2) + 1024]
                            a_nm.inc(v.tensor_reduce(
                                nmx[:, 2 * gi:2 * gi + 2],
                                sl.rearrange("p (h x) -> p h x", h=2),
                                AX.X, ALU.max, negate=True), 1)
                            v.wait_ge(a_ex.h, gg + 1)
                            for hh in range(2):
                                h = 2 * gi + hh
                                last = v.tensor_tensor(
                                    pr2[:, h, :], pr[hp % 2][:, h, :],
                                    msk[:, half, :], ALU.mult)
                        a_fv.inc(last, 1)
                        v.wait_ge(a_fv.h, a_fv.n)
                        a_fv.inc(v.tensor_reduce(sm[:], pr2[:], AX.X, ALU.add), 1)
                        v.wait_ge(a_fv.h, a_fv.n)
                        a_fv.inc(v.reciprocal(rs[:], sm[:]), 1)
                        v.wait_ge(a_fv.h, a_fv.n)
                        for h in range(NH):
                            last = v.tensor_scalar(pr[hp % 2][:, h, :], pr2[:, h, :],
                                                   rs[:, h:h + 1], None, ALU.mult)
                        a_fv.inc(last, 1)
                        v.wait_ge(a_fv.h, a_fv.n)
                        a_nr.inc(v.tensor_copy(dmy[:], rs[:, 0:1]), 1)
                        if hp >= 1:
                            evacs(hp - 1)
                    evacs(2 * NCHUNK - 1)

                @blk.scalar
                def _(sc):
                    for hp in range(2 * NCHUNK):
                        if hp >= 2:
                            sc.wait_ge(a_px.h, 16 * (hp - 1))
                        for gi in range(GH):
                            gg = GH * hp + gi
                            sc.wait_ge(a_nm.h, gg + 1)
                            for hh in range(2):
                                h = 2 * gi + hh
                                sl = ps_s[:, 1024 * (gi % 2) + 512 * hh:
                                          1024 * (gi % 2) + 512 * (hh + 1)]
                                last = sc.activation(pr[hp % 2][:, h, :], sl, AF.Exp,
                                                     bias=nmx[:, h:h + 1], scale=1.0)
                            a_ex.inc(last, 1)

        cst.close()

        if phases >= 7:
            with ExitStack() as pst, nc.Block() as blk:
                pe_ = pst.enter_context
                wbufB = pe_(nc.sbuf_tensor("wbufB", [128, KC, D], BF16))
                wstgB = [pe_(nc.sbuf_tensor(f"wstgB{i}", [128, D], BF16)) for i in range(2)]
                ost = [pe_(nc.sbuf_tensor(f"b_o{i}", [128, KC, 128], BF16)) for i in range(2)]
                xr = [pe_(nc.sbuf_tensor(f"b_x{i}", [128, D], F32)) for i in range(2)]
                nz = pe_(nc.sbuf_tensor("b_nz", [128, D], F32))
                x2t = [pe_(nc.sbuf_tensor(f"b_2{i}", [128, D], F32)) for i in range(2)]
                bn6 = pe_(nc.sbuf_tensor("b_bn", [128, 24], F32))
                mv = pe_(nc.sbuf_tensor("b_mv", [128, 2], F32))
                vep = pe_(nc.sbuf_tensor("b_ve", [128, 1], F32))
                ngm = pe_(nc.sbuf_tensor("b_ng", [128, 1], F32))
                alp = pe_(nc.sbuf_tensor("b_al", [128, 1], F32))
                bet = pe_(nc.sbuf_tensor("b_be", [128, 1], F32))
                lnt = pe_(nc.sbuf_tensor("b_lt", [128, 1], F32))
                ps = [pe_(nc.psum_tensor(f"b_p{i}", [128, D], F32)) for i in range(2)]
                b_w1 = Sem(pst, nc, "b_w1")
                b_w2 = Sem(pst, nc, "b_w2")
                b_x = Sem(pst, nc, "b_x")
                b_pe = Sem(pst, nc, "b_pe")
                b_st = Sem(pst, nc, "b_st")
                b_nm = Sem(pst, nc, "b_nm")
                b_ad = Sem(pst, nc, "b_ad")
                b_so = Sem(pst, nc, "b_so")
                b_fv = Sem(pst, nc, "b_fv")
                b_fs = Sem(pst, nc, "b_fs")

                @blk.gpsimd
                def _(g):
                    for j in range(KC):
                        if j >= 2:
                            g.wait_ge(b_w2.h, 16 * (j - 1))
                        b_w1.inc(g.dma_start(out=wstgB[j % 2][:],
                                             in_=w_att[3][128 * j:128 * (j + 1), :]), 16)
                    for tt in range(TT_OWN):
                        b = tt % 2
                        if tt >= 2:
                            g.wait_ge(b_pe.h, 4 * (tt - 1))
                        b_x.inc(g.dma_start(out=ost[b][:], in_=o_f[:, :, 128 * tt:128 * (tt + 1)]), 16)
                        if tt >= 2:
                            g.wait_ge(b_ad.h, tt - 1)
                        b_x.inc(g.dma_start(out=xr[b][:], in_=x_til[:, tt + 2, :]), 16)

                @blk.sync
                def _(sync):
                    for j in range(KC):
                        sync.wait_ge(b_w1.h, 16 * (j + 1))
                        b_w2.inc(sync.dma_start_transpose(
                            wbufB[:, :, 128 * j:128 * (j + 1)], wstgB[j % 2][:]), 16)
                    for tt in range(TT_OWN):
                        b = tt % 2
                        sync.wait_ge(b_ad.h, tt + 1)
                        if tt >= 2:
                            sync.wait_ge(b_so.h, 16 * (tt - 1))
                        b_so.inc(sync.dma_start(out=x2m[128 * tt:128 * (tt + 1), :], in_=x2t[b][:]), 16)
                    sync.wait_ge(b_so.h, 16 * TT_OWN)

                @blk.tensor
                def _(t):
                    t.wait_ge(b_w2.h, 16 * KC)
                    for tt in range(TT_OWN):
                        b = tt % 2
                        t.wait_ge(b_x.h, 32 * tt + 16)
                        if tt >= 2:
                            t.wait_ge(b_nm.h, tt - 1)
                        for ow in range(4):
                            for kc in range(KC):
                                mm = t.matmul(ps[b][:, 512 * ow:512 * (ow + 1)],
                                              ost[b][:, kc, :],
                                              wbufB[:, kc, 512 * ow:512 * (ow + 1)],
                                              start=(kc == 0), stop=(kc == KC - 1))
                            b_pe.inc(mm, 1)

                @blk.vector
                def _(v):
                    for tt in range(TT_OWN):
                        b = tt % 2
                        v.wait_ge(b_pe.h, 4 * (tt + 1))
                        last = None
                        for q4 in range(4):
                            last = v.bn_stats(bn6[:, 6 * q4:6 * (q4 + 1)],
                                              ps[b][:, 512 * q4:512 * (q4 + 1)])
                        b_fv.inc(last, 1); v.wait_ge(b_fv.h, b_fv.n)
                        b_fv.inc(v.bn_aggr(mv[:], bn6[:]), 1); v.wait_ge(b_fv.h, b_fv.n)
                        v.tensor_scalar(ngm[:], mv[:, 0:1], -1.0, None, ALU.mult)
                        b_st.inc(v.tensor_scalar(vep[:], mv[:, 1:2], 1e-5, None, ALU.add), 1)
                        v.wait_ge(b_nm.h, tt + 1)
                        v.wait_ge(b_x.h, 32 * (tt + 1))
                        if tt >= 2:
                            v.wait_ge(b_so.h, 16 * (tt - 1))
                        b_ad.inc(v.tensor_tensor(x2t[b][:], nz[:], xr[b][:], ALU.add), 1)

                @blk.scalar
                def _(sc):
                    for tt in range(TT_OWN):
                        b = tt % 2
                        sc.wait_ge(b_st.h, tt + 1)
                        b_fs.inc(sc.activation(lnt[:], vep[:], AF.Ln, bias=0.0, scale=1.0), 1)
                        sc.wait_ge(b_fs.h, b_fs.n)
                        b_fs.inc(sc.activation(alp[:], lnt[:], AF.Exp, bias=0.0, scale=-0.5), 1)
                        sc.wait_ge(b_fs.h, b_fs.n)
                        b_fs.inc(sc.activation(bet[:], ngm[:], AF.Identity, bias=0.0, scale=alp[:]), 1)
                        sc.wait_ge(b_fs.h, b_fs.n)
                        if tt >= 1:
                            sc.wait_ge(b_ad.h, tt)
                        b_nm.inc(sc.activation(nz[:], ps[b][:], AF.Identity,
                                               bias=bet[:], scale=alp[:]), 1)

        if phases >= 8:
            with ExitStack() as pst, nc.Block() as blk:
                pe_ = pst.enter_context
                a2f = pe_(nc.sbuf_tensor("f_a2", [128, KC, 512], BF16))
                h_sb = pe_(nc.sbuf_tensor("f_h", [128, IC, 512], BF16))
                wg_g = [pe_(nc.sbuf_tensor(f"f_wg{i}", [128, KC, 128], BF16)) for i in range(3)]
                wg_u = [pe_(nc.sbuf_tensor(f"f_wu{i}", [128, KC, 128], BF16)) for i in range(3)]
                wdl = [pe_(nc.sbuf_tensor(f"f_wd{i}", [128, IC, 128], BF16)) for i in range(3)]
                sg = [pe_(nc.sbuf_tensor(f"f_sg{i}", [128, 512], BF16)) for i in range(2)]
                y_fs = [pe_(nc.sbuf_tensor(f"f_yf{i}", [128, 512], BF16)) for i in range(2)]
                y_tm = pe_(nc.sbuf_tensor("f_yt", [128, 4, D], BF16))
                x2i = [pe_(nc.sbuf_tensor(f"f_x2{i}", [128, D], F32)) for i in range(2)]
                x2r = [pe_(nc.sbuf_tensor(f"f_xr{i}", [128, D], F32)) for i in range(2)]
                a2t = [pe_(nc.sbuf_tensor(f"f_a2t{i}", [128, D], BF16)) for i in range(2)]
                ny = pe_(nc.sbuf_tensor("f_ny", [128, D], F32))
                x3 = [pe_(nc.sbuf_tensor(f"f_x3{i}", [128, D], F32)) for i in range(2)]
                bn6 = pe_(nc.sbuf_tensor("f_bn", [128, 24], F32))
                mv = pe_(nc.sbuf_tensor("f_mv", [128, 2], F32))
                vep = pe_(nc.sbuf_tensor("f_ve", [128, 1], F32))
                ngm = pe_(nc.sbuf_tensor("f_ng", [128, 1], F32))
                alp = pe_(nc.sbuf_tensor("f_al", [128, 1], F32))
                bet = pe_(nc.sbuf_tensor("f_be", [128, 1], F32))
                lnt = pe_(nc.sbuf_tensor("f_lt", [128, 1], F32))
                bn62 = pe_(nc.sbuf_tensor("f_bn2", [128, 24], F32))
                mv2 = pe_(nc.sbuf_tensor("f_mv2", [128, 2], F32))
                ngm2 = pe_(nc.sbuf_tensor("f_ng2", [128, 1], F32))
                alp2 = pe_(nc.sbuf_tensor("f_al2", [128, 1], F32))
                bet2 = pe_(nc.sbuf_tensor("f_be2", [128, 1], F32))
                lnt2 = pe_(nc.sbuf_tensor("f_lt2", [128, 1], F32))
                u1 = pe_(nc.sbuf_tensor("f_u1", [128, 1], F32))
                u2 = pe_(nc.sbuf_tensor("f_u2", [128, 1], F32))
                ps_g = [pe_(nc.psum_tensor(f"f_pg{i}", [128, 512], F32)) for i in range(2)]
                ps_u = [pe_(nc.psum_tensor(f"f_pu{i}", [128, 512], F32)) for i in range(2)]
                ps_y = [pe_(nc.psum_tensor(f"f_py{i}", [128, 512], F32)) for i in range(2)]

                f_x2 = Sem(pst, nc, "f_x2")
                f_st = Sem(pst, nc, "f_st")
                f_nm = Sem(pst, nc, "f_nm")
                f_ax = Sem(pst, nc, "f_ax")
                f_wg = Sem(pst, nc, "f_wg")
                f_wd = Sem(pst, nc, "f_wd")
                f_pe = Sem(pst, nc, "f_pe")
                f_si = Sem(pst, nc, "f_si")
                f_h = Sem(pst, nc, "f_h")
                f_yp = Sem(pst, nc, "f_yp")
                f_ye = Sem(pst, nc, "f_ye")
                f_yx = Sem(pst, nc, "f_yx")
                f_fs = Sem(pst, nc, "f_fs")
                f_fn = Sem(pst, nc, "f_fn")
                f_fa = Sem(pst, nc, "f_fa")
                f_xr = Sem(pst, nc, "f_xr")
                f_fv = Sem(pst, nc, "f_fv")
                f_fc = Sem(pst, nc, "f_fc")

                NT = nt

                @blk.gpsimd
                def _(g):
                    def wg_load(gp, p):
                        if gp >= 3:
                            g.wait_ge(f_pe.h, 2 * (gp - 2))
                        f_wg.inc(g.dma_start(out=wg_g[gp % 3][:], in_=wgu_t[p]), 16)
                        f_wg.inc(g.dma_start(out=wg_u[gp % 3][:], in_=wgu_t[p + IC]), 16)

                    for ti in range(NT):
                        lo = 3 if ti > 0 else 0
                        for p in range(lo, IC):
                            wg_load(IC * ti + p, p)
                        if ti + 1 < NT:
                            for p in range(3):
                                wg_load(IC * (ti + 1) + p, p)
                        for hc in range(HOC):
                            gh = HOC * ti + hc
                            if gh >= 3:
                                g.wait_ge(f_yp.h, gh - 2)
                            f_wd.inc(g.dma_start(out=wdl[gh % 3][:], in_=wd_t[hc]), 16)

                @blk.scalar
                def _(sc):
                    def ln2_group(tj):
                        for stt in range(4):
                            u = 4 * tj + stt
                            if u >= 2:
                                sc.wait_ge(f_nm.h, u - 1)
                            f_x2.inc(sc.dma_start(
                                out=x2i[u % 2][:],
                                in_=x2m[512 * tj + 128 * stt:512 * tj + 128 * (stt + 1), :]), 16)
                            sc.wait_ge(f_st.h, u + 1)
                            f_fc.inc(sc.activation(lnt[:], vep[:], AF.Ln, bias=0.0, scale=1.0), 1)
                            sc.wait_ge(f_fc.h, f_fc.n)
                            f_fc.inc(sc.activation(alp[:], lnt[:], AF.Exp, bias=0.0, scale=-0.5), 1)
                            sc.wait_ge(f_fc.h, f_fc.n)
                            f_fc.inc(sc.activation(bet[:], ngm[:], AF.Identity, bias=0.0, scale=alp[:]), 1)
                            sc.wait_ge(f_fc.h, f_fc.n)
                            if u >= 2:
                                sc.wait_ge(f_ax.h, 16 * (u - 1))
                            a2tv = a2t[u % 2][:].rearrange("t (j p) -> t j p", j=4)
                            x2iv = x2i[u % 2][:].rearrange("t (p j) -> t j p", j=4)
                            f_nm.inc(sc.activation(a2tv, x2iv, AF.Identity,
                                                   bias=bet[:], scale=alp[:]), 1)

                    ln2_group(0)
                    for ti in range(NT):
                        for p in range(IC):
                            gp = IC * ti + p
                            sc.wait_ge(f_pe.h, 2 * gp + 1)
                            if gp >= 2:
                                sc.wait_ge(f_h.h, gp - 1)
                            f_si.inc(sc.activation(sg[gp % 2][:], ps_g[gp % 2][:], AF.Silu), 1)
                        if ti + 1 < NT:
                            ln2_group(ti + 1)
                        for stt in range(4):
                            u = 4 * ti + stt
                            sc.wait_ge(f_fs.h, u + 1)
                            f_fc.inc(sc.activation(lnt2[:], u2[:], AF.Ln, bias=0.0, scale=1.0), 1)
                            sc.wait_ge(f_fc.h, f_fc.n)
                            f_fc.inc(sc.activation(alp2[:], lnt2[:], AF.Exp, bias=0.0, scale=-0.5), 1)
                            sc.wait_ge(f_fc.h, f_fc.n)
                            f_fc.inc(sc.activation(bet2[:], ngm2[:], AF.Identity, bias=0.0, scale=alp2[:]), 1)
                            sc.wait_ge(f_fc.h, f_fc.n)
                            if u >= 1:
                                sc.wait_ge(f_fa.h, u)
                            f_fn.inc(sc.activation(ny[:], y_tm[:, stt, :], AF.Identity,
                                                   bias=bet2[:], scale=alp2[:]), 1)

                @blk.vector
                def _(v):
                    def stats_group(tj):
                        for stt in range(4):
                            u = 4 * tj + stt
                            v.wait_ge(f_x2.h, 16 * (u + 1))
                            last = None
                            for q4 in range(4):
                                last = v.bn_stats(bn6[:, 6 * q4:6 * (q4 + 1)],
                                                  x2i[u % 2][:, 512 * q4:512 * (q4 + 1)])
                            f_fv.inc(last, 1); v.wait_ge(f_fv.h, f_fv.n)
                            f_fv.inc(v.bn_aggr(mv[:], bn6[:]), 1); v.wait_ge(f_fv.h, f_fv.n)
                            v.tensor_scalar(ngm[:], mv[:, 0:1], -1.0, None, ALU.mult)
                            f_st.inc(v.tensor_scalar(vep[:], mv[:, 1:2], 1e-5, None, ALU.add), 1)

                    stats_group(0)
                    for ti in range(NT):
                        for p in range(IC):
                            gp = IC * ti + p
                            if ti >= 1 and p == 0:
                                v.wait_ge(f_yp.h, HOC * ti)
                            v.wait_ge(f_si.h, gp + 1)
                            v.wait_ge(f_pe.h, 2 * gp + 2)
                            f_h.inc(v.tensor_tensor(h_sb[:, p, :], sg[gp % 2][:],
                                                    ps_u[gp % 2][:], ALU.mult), 1)
                        if ti + 1 < NT:
                            stats_group(ti + 1)
                        for hc in range(HOC):
                            gh = HOC * ti + hc
                            v.wait_ge(f_yp.h, gh + 1)
                            if gh >= 2:
                                v.wait_ge(f_yx.h, 16 * (gh - 1))
                            f_ye.inc(v.tensor_copy(y_fs[gh % 2][:], ps_y[gh % 2][:]), 1)
                        for stt in range(4):
                            u = 4 * ti + stt
                            v.wait_ge(f_yx.h, 16 * HOC * (ti + 1))
                            last = None
                            for q4 in range(4):
                                last = v.bn_stats(bn62[:, 6 * q4:6 * (q4 + 1)],
                                                  y_tm[:, stt, 512 * q4:512 * (q4 + 1)])
                            f_fv.inc(last, 1); v.wait_ge(f_fv.h, f_fv.n)
                            f_fv.inc(v.bn_aggr(mv2[:], bn62[:]), 1); v.wait_ge(f_fv.h, f_fv.n)
                            v.tensor_scalar(ngm2[:], mv2[:, 0:1], -1.0, None, ALU.mult)
                            f_fv.inc(v.tensor_scalar(u1[:], mv2[:, 0:1], mv2[:, 0:1], mv2[:, 1:2],
                                                     ALU.mult, ALU.add), 1)
                            v.wait_ge(f_fv.h, f_fv.n)
                            f_fv.inc(v.tensor_scalar(u2[:], u1[:], 1e-5, mv2[:, 1:2],
                                                     ALU.mult, ALU.add), 1)
                            v.wait_ge(f_fv.h, f_fv.n)
                            f_fs.inc(v.tensor_scalar(u2[:], u2[:], 1e-11, None, ALU.add), 1)
                            v.wait_ge(f_fn.h, u + 1)
                            v.wait_ge(f_xr.h, 16 * (u + 1))
                            if u >= 2:
                                v.wait_ge(s_fin.h, 16 * (u - 1))
                            f_fa.inc(v.tensor_tensor(x3[u % 2][:], ny[:], x2r[u % 2][:], ALU.add), 1)

                @blk.sync
                def _(sync):
                    def a2_xbars(tj):
                        for stt in range(4):
                            u = 4 * tj + stt
                            sync.wait_ge(f_nm.h, u + 1)
                            if tj >= 1 and stt == 0:
                                sync.wait_ge(f_pe.h, 2 * IC * tj)
                            f_ax.inc(sync.dma_start_transpose(
                                a2f[:, :, 128 * stt:128 * (stt + 1)], a2t[u % 2][:]), 16)

                    a2_xbars(0)
                    for ti in range(NT):
                        if ti + 1 < NT:
                            a2_xbars(ti + 1)
                        for hc in range(HOC):
                            gh = HOC * ti + hc
                            sync.wait_ge(f_ye.h, gh + 1)
                            if ti >= 1 and hc == 0:
                                sync.wait_ge(f_fn.h, 4 * ti)
                            f_yx.inc(sync.dma_start_transpose(
                                y_tm[:, :, 128 * hc:128 * (hc + 1)], y_fs[gh % 2][:]), 16)
                        for stt in range(4):
                            u = 4 * ti + stt
                            if u >= 2:
                                sync.wait_ge(f_fa.h, u - 1)
                            f_xr.inc(sync.dma_start(
                                out=x2r[u % 2][:],
                                in_=x2m[512 * ti + 128 * stt:512 * ti + 128 * (stt + 1), :]), 16)
                        for stt in range(4):
                            u = 4 * ti + stt
                            sync.wait_ge(f_fa.h, u + 1)
                            s_fin.inc(sync.dma_start(
                                out=out[512 * ti + 128 * stt:512 * ti + 128 * (stt + 1), :],
                                in_=x3[u % 2][:]), 16)
                    sync.wait_ge(s_fin.h, 16 * 4 * NT)

                @blk.tensor
                def _(t):
                    for ti in range(NT):
                        t.wait_ge(f_ax.h, 16 * 4 * (ti + 1))
                        for p in range(IC):
                            gp = IC * ti + p
                            t.wait_ge(f_wg.h, 32 * (gp + 1))
                            if gp >= 2:
                                t.wait_ge(f_si.h, gp - 1)
                                t.wait_ge(f_h.h, gp - 1)
                            for kc in range(KC):
                                mm = t.matmul(ps_g[gp % 2][:], wg_g[gp % 3][:, kc, :],
                                              a2f[:, kc, :], start=(kc == 0), stop=(kc == KC - 1))
                            f_pe.inc(mm, 1)
                            for kc in range(KC):
                                mm = t.matmul(ps_u[gp % 2][:], wg_u[gp % 3][:, kc, :],
                                              a2f[:, kc, :], start=(kc == 0), stop=(kc == KC - 1))
                            f_pe.inc(mm, 1)
                        for hc in range(HOC):
                            gh = HOC * ti + hc
                            t.wait_ge(f_wd.h, 16 * (gh + 1))
                            t.wait_ge(f_h.h, IC * (ti + 1))
                            if gh >= 2:
                                t.wait_ge(f_ye.h, gh - 1)
                            for ic in range(IC):
                                mm = t.matmul(ps_y[gh % 2][:], wdl[gh % 3][:, ic, :],
                                              h_sb[:, ic, :], start=(ic == 0), stop=(ic == IC - 1))
                            f_yp.inc(mm, 1)

        st.close()
        return nc
    return nc


def _host_prep(L0_attn_x, Wq, Wk, Wv, Wo, gup_a, gus_a, dnp_a, dns_a):
    x = np.asarray(L0_attn_x, dtype=np.float32).reshape(16384, 2048)
    inv = 1.0 / (10000.0 ** (np.arange(0, 128, 2, dtype=np.float64) / 128.0))
    dns_pad = np.zeros((48, 2048), dtype=np.float16)
    dns_pad[0:44] = np.asarray(dns_a, dtype=np.float16)
    scale = 1.0 / np.sqrt(128.0)
    Wq = np.ascontiguousarray(np.asarray(Wq, np.float32))
    Wk = np.ascontiguousarray(np.asarray(Wk, np.float32))
    Wv = np.ascontiguousarray(np.asarray(Wv, np.float32))
    Wo = np.ascontiguousarray(np.asarray(Wo, np.float32))
    gup_a = np.ascontiguousarray(np.asarray(gup_a, np.int32)
                                 .reshape(2, 1408, 4, 256).transpose(0, 2, 1, 3)
                                 .reshape(11264, 256))
    gus_a = np.ascontiguousarray(np.asarray(gus_a, np.float16)
                                 .reshape(16, 2, 1408, 4).transpose(0, 1, 3, 2)
                                 .reshape(16, 11264))
    dnp_a = np.ascontiguousarray(np.asarray(dnp_a, np.int32))
    tril = np.tril(np.ones((256, 256), np.float32))
    base = np.zeros((256, 512), np.float32)
    base[:, 0:256] = 1.0
    base[:, 256:512] = tril
    first = np.zeros((256, 512), np.float32)
    first[:, 256:512] = tril
    mB_ = np.ascontiguousarray(base.reshape(2, 128, 512).transpose(1, 0, 2))
    mF_ = np.ascontiguousarray(first.reshape(2, 128, 512).transpose(1, 0, 2))
    in_maps = []
    for c in range(NCORES):
        lo = c * T_OWN
        halo = np.zeros((256, 2048), np.float32) if c == 0 else x[lo - 256:lo]
        x_slice = np.ascontiguousarray(np.concatenate([halo, x[lo:lo + T_OWN]], axis=0))
        pos_all = np.arange(lo - 256, lo + T_OWN, dtype=np.float64)
        if c == 0:
            pos_all = np.maximum(pos_all, 0.0)
        ang = np.outer(pos_all, inv)
        ck_t = np.ascontiguousarray(np.cos(ang).astype(np.float32).reshape(TT_ALL, 128, 64).transpose(1, 0, 2))
        sk_t = np.ascontiguousarray(np.sin(ang).astype(np.float32).reshape(TT_ALL, 128, 64).transpose(1, 0, 2))
        ang_o = ang[256:]
        cq_t = np.ascontiguousarray((np.cos(ang_o) * scale).astype(np.float32).reshape(TT_OWN, 128, 64).transpose(1, 0, 2))
        sq_t = np.ascontiguousarray((np.sin(ang_o) * scale).astype(np.float32).reshape(TT_OWN, 128, 64).transpose(1, 0, 2))
        in_maps.append({
            "x": x_slice, "wq": Wq, "wk": Wk, "wv": Wv, "wo": Wo,
            "gup": gup_a, "gus": gus_a, "dnp": dnp_a, "dns": dns_pad,
            "cq": cq_t, "sq": sq_t, "ck": ck_t, "sk": sk_t,
            "m0": (mF_ if c == 0 else mB_), "mB": mB_,
        })
    return in_maps


def kernel(L0_attn_x, L0_attn_Wq, L0_attn_Wk, L0_attn_Wv, L0_attn_Wo,
           L0_ffn_gate_up_packed, L0_ffn_gate_up_scales,
           L0_ffn_down_packed, L0_ffn_down_scales):
    in_maps = _host_prep(L0_attn_x, L0_attn_Wq, L0_attn_Wk, L0_attn_Wv, L0_attn_Wo,
                         L0_ffn_gate_up_packed, L0_ffn_gate_up_scales,
                         L0_ffn_down_packed, L0_ffn_down_scales)
    nc = build()
    res = run_bass_kernel_spmd(nc, in_maps, core_ids=list(range(NCORES)))
    out = np.concatenate([r["out"] for r in res.results], axis=0)
    return out.reshape(1, 16384, 2048)
